# version 06
# baseline (speedup 1.0000x reference)
import contextlib
import numpy as np
import ml_dtypes
import concourse.bass as bass
import concourse.mybir as mybir
from concourse.bass_utils import run_bass_kernel_spmd

F32 = mybir.dt.float32
BF16 = mybir.dt.bfloat16
AF = mybir.ActivationFunctionType
ALU = mybir.AluOpType
AX = mybir.AxisListType
NPBF = np.dtype(ml_dtypes.bfloat16)

DEBUG_TAPS = False
BUILD_PHASES = 99
NSEQ = 2
SERIAL_DMA = False
SSD_LIMIT = 99

D = 1024
DI = 2048
NH = 32
HD = 64
CONVD = 2560
CK = 128
NCH = 16
S = 2048
EPS = 1e-5
HA = 16
NE = 8
NTC = 16

ENGS = ("tensor", "vector", "scalar", "gpsimd", "sync")


class Op:
    __slots__ = ("eng", "fn", "dma", "idx", "needs_inc", "deps", "tag",
                 "lane", "lane_idx", "dur", "est", "pos")


class Sched:
    NL = 8

    def __init__(self):
        self.ops = {e: [] for e in ENGS}
        self.wr = {}
        self.rd = {}
        self.psum_names = set()
        self.dma_count = {e: 0 for e in ENGS}
        self.lane_counts = {e: [0] * self.NL for e in ENGS}

    def _rng(self, ap):
        lo = ap.offset
        hi = ap.offset
        for st, ct in ap.ap:
            if st >= 0:
                hi += st * (ct - 1)
            else:
                lo += st * (ct - 1)
        hi += 1
        esz = mybir.dt.size(ap.dtype)
        lo *= esz
        hi *= esz
        name = ap.tensor.name
        if name in self.psum_names:
            lo = (lo // 2048) * 2048
            hi = ((hi + 2047) // 2048) * 2048
        return name, lo, hi

    def add(self, eng, fn, reads=(), writes=(), dma=False, tag=""):
        op = Op()
        op.eng, op.fn, op.dma, op.tag = eng, fn, dma, tag
        op.idx = len(self.ops[eng])
        op.needs_inc = False
        op.lane = -1
        op.lane_idx = 0
        deps = {}

        def want(rec):
            _, _, deng, didx, ddma = rec
            deps.setdefault(deng, set()).add(didx)

        rrs = [self._rng(ap) for ap in reads]
        wrs = [self._rng(ap) for ap in writes]
        for name, lo, hi in rrs:
            for rec in self.wr.get(name, ()):
                if rec[0] < hi and lo < rec[1]:
                    want(rec)
        for name, lo, hi in wrs:
            for rec in self.wr.get(name, ()):
                if rec[0] < hi and lo < rec[1]:
                    want(rec)
            for rec in self.rd.get(name, ()):
                if rec[0] < hi and lo < rec[1]:
                    want(rec)
        for name, lo, hi in rrs:
            self.rd.setdefault(name, []).append((lo, hi, eng, op.idx, dma))
        for name, lo, hi in wrs:
            wl = self.wr.get(name)
            if wl:
                self.wr[name] = [r for r in wl if not (lo <= r[0] and r[1] <= hi)]
            rl = self.rd.get(name)
            if rl:
                self.rd[name] = [r for r in rl if not (lo <= r[0] and r[1] <= hi)]
            self.wr.setdefault(name, []).append((lo, hi, eng, op.idx, dma))
        op.deps = deps
        fb = 64
        for ap in writes:
            n, lo, hi = self._rng(ap)
            fb = max(fb, (hi - lo) // 128)
        if dma:
            op.dur = 1400 + fb * 128 // 300
        else:
            kk = {"tensor": 0.11, "vector": 0.30, "scalar": 0.25,
                  "gpsimd": 0.5, "sync": 0.1}[eng]
            op.dur = int(150 + fb * kk)
        est = 0
        for deng, dids in deps.items():
            for di in dids:
                d = self.ops[deng][di]
                v = d.est + d.dur + (0 if deng == eng else 500)
                if v > est:
                    est = v
        op.est = est
        self.ops[eng].append(op)
        return (eng, op.idx)

    def materialize(self, nc, blk, sems):
        NL = self.NL
        order = {}
        for e in ENGS:
            order[e] = list(range(len(self.ops[e])))
            for pos, i in enumerate(order[e]):
                self.ops[e][i].pos = pos
        for e in ENGS:
            self.lane_counts[e] = [0] * NL
            nd = 0
            for i in order[e]:
                op = self.ops[e][i]
                if op.dma:
                    op.lane = nd % NL
                    nd += 1
                    self.lane_counts[e][op.lane] += 1
                    op.lane_idx = self.lane_counts[e][op.lane]
        for e in ENGS:
            for op in self.ops[e]:
                red = {}
                for deng, dids in op.deps.items():
                    best = {}
                    for di in dids:
                        d = self.ops[deng][di]
                        k = d.lane if d.dma else -1
                        if k not in best or self.ops[deng][best[k]].pos < d.pos:
                            best[k] = di
                    for k, di in best.items():
                        red[(deng, k)] = di
                        if k == -1:
                            self.ops[deng][di].needs_inc = True
                op.deps = red
        exec_sems = {e: sems[f"x_{e}"] for e in ENGS}
        lane_sems = {e: [sems[f"d_{e}{l}"] for l in range(NL)] for e in ENGS}
        exec_pre = {}
        for e in ENGS:
            xp = [0] * len(self.ops[e])
            cx = 0
            for i in order[e]:
                if (not self.ops[e][i].dma) and self.ops[e][i].needs_inc:
                    cx += 1
                xp[i] = cx
            exec_pre[e] = xp
        sched = self

        def body_for(ename):
            def body(engobj):
                waited = {}
                for i in order[ename]:
                    op = sched.ops[ename][i]
                    for (deng, lane), didx in sorted(op.deps.items()):
                        dop = sched.ops[deng][didx]
                        if lane >= 0:
                            val = 16 * dop.lane_idx
                            sem = lane_sems[deng][lane]
                        else:
                            val = exec_pre[deng][didx]
                            sem = exec_sems[deng]
                        k = (deng, lane)
                        if waited.get(k, 0) < val:
                            engobj.wait_ge(sem, val)
                            waited[k] = val
                    if op.dma:
                        if op.lane_idx > 1:
                            v = 16 * (op.lane_idx - 1)
                            if waited.get((ename, op.lane), 0) < v:
                                engobj.wait_ge(lane_sems[ename][op.lane], v)
                                waited[(ename, op.lane)] = v
                        ins = op.fn(engobj)
                        ins.then_inc(lane_sems[ename][op.lane], 16)
                    else:
                        ins = op.fn(engobj)
                        if op.needs_inc:
                            ins.then_inc(exec_sems[ename], 1)
                for l in range(NL):
                    if sched.lane_counts[ename][l]:
                        engobj.wait_ge(lane_sems[ename][l],
                                       16 * sched.lane_counts[ename][l])
            return body

        blk.tensor(body_for("tensor"))
        blk.vector(body_for("vector"))
        blk.scalar(body_for("scalar"))
        blk.gpsimd(body_for("gpsimd"))
        blk.sync(body_for("sync"))


class View:

    def __init__(self, handle, aw, off, P, F, dtype):
        self.h, self.aw, self.off = handle, aw, off
        self.P, self.F, self.dtype = P, F, dtype
        self.esz = 1 if dtype == BF16 else 2

    def sl(self, p0, pn, f0, fn, fdims=None):
        e = self.esz
        if fdims is None:
            fdims = [(1, fn)]
        if self.dtype == F32:
            st_l, ct_l = fdims[-1]
            if st_l == 1:
                dims = ([[self.aw, pn]]
                        + [[st * e, ct] for st, ct in fdims[:-1]]
                        + [[1, ct_l * e]])
            else:
                assert st_l == 0, "f32 view last dim must be step 0 or 1"
                dims = ([[self.aw, pn]]
                        + [[st * e, ct] for st, ct in fdims[:-1]]
                        + [[0, ct_l], [1, e]])
            a = bass.AP(self.h, p0 * self.aw + self.off + f0 * e, dims)
            return a.bitcast(F32)
        dims = [[self.aw, pn]] + [[st, ct] for st, ct in fdims]
        return bass.AP(self.h, p0 * self.aw + self.off + f0, dims)

    def __getitem__(self, idx):
        p, f = idx
        p0 = p.start or 0
        pn = (p.stop if p.stop is not None else self.P) - p0
        f0 = f.start or 0
        fn = (f.stop if f.stop is not None else self.F) - f0
        return self.sl(p0, pn, f0, fn)


class PView:

    def __init__(self, handle, off, P, F):
        self.h, self.off, self.P, self.F = handle, off, P, F
        self.aw = 4096

    def sl(self, p0, pn, f0, fn, fdims=None):
        if fdims is None:
            fdims = [(1, fn)]
        dims = [[self.aw, pn]] + [[st, ct] for st, ct in fdims]
        return bass.AP(self.h, p0 * self.aw + self.off + f0, dims)

    def __getitem__(self, idx):
        p, f = idx
        p0 = p.start or 0
        pn = (p.stop if p.stop is not None else self.P) - p0
        f0 = f.start or 0
        fn = (f.stop if f.stop is not None else self.F) - f0
        return self.sl(p0, pn, f0, fn)

    def bf(self, p0, pn, f32_off, n_bf):
        assert n_bf % 2 == 0
        a = bass.AP(self.h, p0 * self.aw + self.off + f32_off,
                    [[self.aw, pn], [1, n_bf // 2]])
        return a.bitcast(BF16)


class Arena:
    def __init__(self, handle, total_bf):
        self.h, self.aw = handle, total_bf
        self.live = {}

    def alloc(self, name, P, F, dtype):
        assert name not in self.live
        size = F * (1 if dtype == BF16 else 2)
        off = 0
        for o, s in sorted(self.live.values()):
            if off + size <= o:
                break
            off = max(off, o + s)
        off = (off + 1) // 2 * 2
        assert off + size <= self.aw, f"arena OOM {name}: {off}+{size}>{self.aw}"
        self.live[name] = (off, size)
        return View(self.h, self.aw, off, P, F, dtype)

    def free(self, *names):
        for n in names:
            del self.live[n]


def host_prep(inputs):
    g = {k: np.ascontiguousarray(np.asarray(v, np.float32)) for k, v in inputs.items()}
    W = {}
    ipw = g["L0_attn_in_proj_w"]
    W["W_zdt"] = np.ascontiguousarray(
        np.concatenate([ipw[:DI].T, ipw[DI + CONVD:].T], axis=1))
    W["W_xbc"] = np.ascontiguousarray(ipw[DI:DI + CONVD].T)
    cw = g["L0_attn_conv_w"][:, 0, :]
    W["CWT"] = np.ascontiguousarray(cw.reshape(20, 128, 4).transpose(1, 0, 2).reshape(128, 80))
    W["CBC"] = np.ascontiguousarray(g["L0_attn_conv_b"].reshape(20, 128).T)
    dtb = np.concatenate([g["L0_attn_dt_bias"], g["L0_attn_dt_bias"]])
    W["DTB"] = np.ascontiguousarray(np.broadcast_to(dtb[None, :], (128, 64)))
    W["ALOGR"] = np.ascontiguousarray(np.broadcast_to(g["L0_attn_A_log"][None, :], (128, 32)))
    W["DREP"] = np.ascontiguousarray(np.broadcast_to(g["L0_attn_D"][None, :], (128, 32)))
    W["W_fcd"] = np.ascontiguousarray(
        g["L0_attn_fc_D_w"].T.reshape(16, 128, 32).transpose(1, 0, 2).reshape(128, 512))
    W["NORMW"] = np.ascontiguousarray(g["L0_attn_norm_w"].reshape(16, 128).T)
    W["W_out"] = np.ascontiguousarray(g["L0_attn_out_proj_w"].T)
    W["W_q"] = g["L0_ffn_Wq"]
    W["W_k"] = g["L0_ffn_Wk"]
    W["W_v"] = g["L0_ffn_Wv"]
    W["W_o"] = g["L0_ffn_Wo"]
    W["BQC"] = np.ascontiguousarray(g["L0_ffn_bq"].reshape(8, 128).T)
    W["BKC"] = np.ascontiguousarray(g["L0_ffn_bk"].reshape(8, 128).T)
    W["BVROW"] = np.ascontiguousarray(g["L0_ffn_bv"][None, :].astype(NPBF))
    W["BOROW"] = np.ascontiguousarray(g["L0_ffn_bo"][None, :].astype(NPBF))
    W["W_g"] = g["L0_ffn_Wg"]
    W["BGR"] = np.ascontiguousarray(np.broadcast_to(g["L0_ffn_bg"][None, :], (128, 8)))
    W["W_e"] = g["L0_ffn_We"]
    W["BEXP"] = np.ascontiguousarray(g["L0_ffn_bexp"].astype(NPBF))
    for nm, key in (("G1R", "L0_ffn_g1"), ("BE1R", "L0_ffn_be1"),
                    ("G2R", "L0_ffn_g2"), ("BE2R", "L0_ffn_be2")):
        W[nm] = np.ascontiguousarray(np.broadcast_to(g[key][None, :], (128, D)))

    k = np.arange(128)
    t = np.arange(128)
    TRI = np.zeros((128, 512), np.float32)
    TRI[:, 0:128] = -(k[:, None] <= t[None, :]).astype(np.float32)
    TRI[:, 128:256] = -(k[:, None] > t[None, :]).astype(np.float32)
    TRI[:, 256:384] = -(k[:, None] >= t[None, :]).astype(np.float32)
    TRI[:, 384:512] = -(k[:, None] < t[None, :]).astype(np.float32)
    W["TRI"] = TRI
    MASK = np.zeros((128, 256), np.float32)
    MASK[:, 0:128] = (t[None, :] >= k[:, None]).astype(np.float32)
    MASK[:, 128:256] = (t[None, :] <= k[:, None]).astype(np.float32)
    W["MASK"] = MASK.astype(NPBF)
    W["IDT32"] = np.eye(128, dtype=np.float32)
    W["IDT16"] = np.eye(128, dtype=np.float32).astype(NPBF)
    BD = np.zeros((64, 4096), np.float32)
    for h in range(32):
        BD[h, h * 128:(h + 1) * 128] = 1.0
        BD[32 + h, h * 128:(h + 1) * 128] = 1.0
    W["BDIAG"] = BD.astype(NPBF)
    W["ONEB"] = np.ones((1, 128), NPBF)
    W["ONEF"] = np.ones((1, 128), np.float32)
    OH = np.zeros((128, 128), np.float32)
    for kc in range(8):
        for p in range(128):
            OH[p, kc * 16 + kc * 2 + (p // 64)] = 1.0
    W["OH"] = OH.astype(NPBF)
    W["EPSC"] = np.ascontiguousarray(
        np.stack([np.full(128, EPS, np.float32), np.ones(128, np.float32)], 1))
    return W


HOST_KEYS = ["W_zdt", "W_xbc", "CWT", "CBC", "DTB", "ALOGR", "DREP", "W_fcd",
             "NORMW", "W_out", "W_q", "W_k", "W_v", "W_o", "BQC", "BKC",
             "BVROW", "BOROW", "W_g", "BGR", "W_e", "BEXP", "G1R", "BE1R",
             "G2R", "BE2R", "TRI", "MASK", "IDT32", "IDT16", "BDIAG", "ONEB",
             "ONEF", "OH", "EPSC"]


def build_program():
    nc = bass.Bass("TRN2", target_bir_lowering=False)
    sch = Sched()
    stack = contextlib.ExitStack()

    def din(name, shape, dtype=F32):
        return nc.dram_tensor(name, list(shape), dtype, kind="ExternalInput").ap()

    u = din("u", (2 * S, D))
    dr = {}
    shapes = dict(W_zdt=(D, 2112), W_xbc=(D, CONVD), CWT=(128, 80), CBC=(128, 20),
                  DTB=(128, 64), ALOGR=(128, 32), DREP=(128, 32), W_fcd=(128, 512),
                  NORMW=(128, 16), W_out=(DI, D), W_q=(D, D), W_k=(D, D),
                  W_v=(D, D), W_o=(D, D), BQC=(128, 8), BKC=(128, 8),
                  BVROW=(1, D), BOROW=(1, D), W_g=(D, 8), BGR=(128, 8),
                  W_e=(NE, D, D), BEXP=(NE, D), G1R=(128, D), BE1R=(128, D),
                  G2R=(128, D), BE2R=(128, D), TRI=(128, 512), MASK=(128, 256),
                  IDT32=(128, 128), IDT16=(128, 128), BDIAG=(64, 4096),
                  ONEB=(1, 128), ONEF=(1, 128), OH=(128, 128), EPSC=(128, 2))
    bf16keys = {"MASK", "IDT16", "BDIAG", "ONEB", "OH", "BVROW", "BOROW", "BEXP"}
    for kk in HOST_KEYS:
        dr[kk] = din(kk, shapes[kk], BF16 if kk in bf16keys else F32)

    out = nc.dram_tensor("out", [2 * S, D], F32, kind="ExternalOutput").ap()
    kind_sp = "ExternalOutput" if DEBUG_TAPS else "Internal"
    z_sp = nc.dram_tensor("z_sp", [S, DI], BF16, kind=kind_sp).ap()
    xc_sp = nc.dram_tensor("xc_sp", [S, DI], BF16, kind=kind_sp).ap()
    yfw_sp = nc.dram_tensor("yfw_sp", [S, DI], BF16, kind=kind_sp).ap()
    x1_sp = nc.dram_tensor("x1_sp", [S, D], BF16, kind=kind_sp).ap()
    x2_sp = nc.dram_tensor("x2_sp", [S, D], BF16, kind=kind_sp).ap()
    ctx_sp = nc.dram_tensor("ctx_sp", [NTC, 8, 128, 128], BF16, kind=kind_sp).ap()
    dbg_sp = nc.dram_tensor("dbg_sp", [64, 128, 2048], F32, kind=kind_sp).ap() \
        if DEBUG_TAPS else None
    dbg_ct = [0]

    def sbuf(name, shape, dtype):
        return stack.enter_context(nc.sbuf_tensor(name, list(shape), dtype))

    C = {}
    cshape = dict(TRI=((128, 512), F32), MASK=((128, 256), BF16),
                  IDT32=((128, 128), F32), IDT16=((128, 128), BF16),
                  ONEB=((1, 128), BF16), ONEF=((1, 128), F32),
                  OH=((128, 128), BF16), CWT=((128, 80), F32),
                  CBC=((128, 20), F32), DTB=((128, 64), F32),
                  ALOGR=((128, 32), F32), DREP=((128, 32), F32),
                  W_fcd=((128, 512), F32), NORMW=((128, 16), F32),
                  BQC=((128, 8), F32), BKC=((128, 8), F32), BGR=((128, 8), F32),
                  BVROW=((1, D), BF16), BOROW=((1, D), BF16),
                  BEXP=((NE, D), BF16), G1R=((128, D), F32),
                  BE1R=((128, D), F32), G2R=((128, D), F32), BE2R=((128, D), F32),
                  EPSC=((128, 2), F32))
    for kk, (shp, dt_) in cshape.items():
        C[kk] = sbuf("c_" + kk, shp, dt_)
    RHS_D = sbuf("RHS_D", (68, 4096), BF16)
    LHSDA = sbuf("LHSDA", (68, 128), BF16)
    LHSDB = sbuf("LHSDB", (68, 128), BF16)
    TOTROW = sbuf("TOTROW", (1, 64), F32)
    MHFLAT = sbuf("MHFLAT", (1, 16), F32)

    AR_BF = 84 * 1024
    ARh = sbuf("AR", (128, AR_BF), BF16)
    ar = Arena(ARh, AR_BF)
    PSh = stack.enter_context(nc.psum_tensor("PS", [128, 4096], F32))
    sch.psum_names.add(PSh[:, :].tensor.name)

    def psv(bank, nbanks=1, P=128):
        return PView(PSh, bank * 512, P, nbanks * 512)

    def T(fn, reads=(), writes=(), tag=""):
        return sch.add("tensor", fn, reads, writes, tag=tag)

    def V(fn, reads=(), writes=(), tag=""):
        return sch.add("vector", fn, reads, writes, tag=tag)

    def SC(fn, reads=(), writes=(), tag=""):
        return sch.add("scalar", fn, reads, writes, tag=tag)

    def DMA(q, dst, src, tag=""):
        return sch.add(q, lambda e, dd=dst, ss=src: e.dma_start(out=dd, in_=ss),
                       reads=[src], writes=[dst], dma=True, tag=tag)

    def mm(outap, lhs, rhs, start, stop, tag=""):
        return T(lambda e, o=outap, a=lhs, b=rhs, st=start, sp=stop:
                 e.matmul(o, a, b, start=st, stop=sp),
                 reads=[lhs, rhs], writes=[outap], tag=tag)

    def trans(outap, inap, ident, tag=""):
        return T(lambda e, o=outap, i=inap, dd=ident: e.transpose(o, i, dd),
                 reads=[inap, ident], writes=[outap], tag=tag)

    def vcopy(dst, src, tag=""):
        return V(lambda e, dd=dst, ss=src: e.tensor_copy(dd, ss),
                 reads=[src], writes=[dst], tag=tag)

    def vtt(dst, a, b, op, tag=""):
        return V(lambda e, dd=dst, x=a, y=b, o=op: e.tensor_tensor(dd, x, y, o),
                 reads=[a, b], writes=[dst], tag=tag)

    def vts(dst, a, s1, s2, op0, op1=ALU.bypass, tag=""):
        rd = [a] + [x for x in (s1, s2) if isinstance(x, bass.AP)]
        return V(lambda e, dd=dst, x=a, uu=s1, vv=s2, o0=op0, o1=op1:
                 e.tensor_scalar(dd, x, uu, vv, o0, o1),
                 reads=rd, writes=[dst], tag=tag)

    def vstt(dst, a, scal, b, op0, op1, tag=""):
        rd = [a, b] + ([scal] if isinstance(scal, bass.AP) else [])
        return V(lambda e, dd=dst, x=a, s=scal, y=b, o0=op0, o1=op1:
                 e.scalar_tensor_tensor(dd, x, s, y, o0, o1),
                 reads=rd, writes=[dst], tag=tag)

    def vred(dst, src, op, tag=""):
        return V(lambda e, dd=dst, ss=src, o=op: e.tensor_reduce(dd, ss, AX.X, o),
                 reads=[src], writes=[dst], tag=tag)

    def vrecip(dst, src, tag=""):
        return V(lambda e, dd=dst, ss=src: e.reciprocal(dd, ss),
                 reads=[src], writes=[dst], tag=tag)

    nc.const_aps.register(F32, EPS, C["EPSC"][0:128, 0:1])
    nc.const_aps.register(F32, 1.0, C["EPSC"][0:128, 1:2])

    def act(dst, src, func, bias=0.0, scale=1.0, accum=None, tag=""):
        rd = [src] + [x for x in (bias, scale) if isinstance(x, bass.AP)]
        if isinstance(bias, float) and bias != 0.0:
            rd.append(C["EPSC"][0:128, 0:2])
        wr = [dst] + ([accum] if accum is not None else [])
        return SC(lambda e, dd=dst, ss=src, f=func, b=bias, sc=scale, a=accum:
                  e.activation(dd, ss, f, bias=b, scale=sc, accum_out=a),
                  reads=rd, writes=wr, tag=tag)

    def dbg_dump(ap_src, tag=""):
        if dbg_sp is None or dbg_ct[0] >= 64:
            return
        slot = dbg_ct[0]
        dbg_ct[0] += 1
        DMA("sync", bass.AP(dbg_sp.tensor, slot * 128 * 2048,
                            [[2048, 128], [1, ap_src.ap[-1][1]]]), ap_src,
            tag=f"dbg{slot}.{tag}")

    def memset(apx, val, tag=""):
        return V(lambda e, a=apx, v=val: e.memset(a, v),
                 reads=[], writes=[apx], tag=tag)

    def hbm(t_ap, width, row0, nrows, col0=0, ncols=None):
        if ncols is None:
            ncols = width
        return bass.AP(t_ap.tensor, row0 * width + col0, [[width, nrows], [1, ncols]])

    for kk in cshape:
        DMA("sync", C[kk][0:cshape[kk][0][0], 0:cshape[kk][0][1]], dr[kk], tag=f"c_{kk}")
    DMA("sync", RHS_D[0:64, :], dr["BDIAG"], tag="c_bdiag")
    AEXP = C["ALOGR"]
    act(AEXP[:, :], AEXP[:, :], AF.Exp, tag="Aexp")
    memset(bass.AP(LHSDA, 64 * 128, [[128, 4], [1, 128]]), 0.0, tag="lhsdA0")
    memset(bass.AP(LHSDA, 64 * 128, [[128, 2], [1, 128]]), 1.0, tag="lhsdA1")
    memset(bass.AP(LHSDB, 64 * 128, [[128, 4], [1, 128]]), 1.0, tag="lhsdB1")
    memset(bass.AP(LHSDB, 64 * 128, [[128, 2], [1, 128]]), 0.0, tag="lhsdB0")
    memset(RHS_D[64:68, :], 0.0, tag="rhsd0")

    def load_w(view, src_name, rows, cols, tag=""):
        srcap = dr[src_name]
        for kc in range(rows // 128):
            dst = view[0:128, kc * cols:(kc + 1) * cols]
            src = bass.AP(srcap.tensor, srcap.offset + kc * 128 * cols,
                          [[cols, 128], [1, cols]])
            DMA("gpsimd", dst, src, tag=f"{tag}{kc}")

    lnsc = ar.alloc("lnsc", 128, DI, BF16)
    lnst = ar.alloc("lnst", 128, 16, F32)

    def layernorm(dst_ap, src_ap, F, tag="", g=None, b=None, tmp=None):
        s1 = lnst[0:128, 0:1]
        s2 = lnst[0:128, 1:2]
        m = lnst[0:128, 2:3]
        vv = lnst[0:128, 3:4]
        r = lnst[0:128, 4:5]
        tt = lnst[0:128, 5:6]
        vred(s1, src_ap, ALU.add, tag=f"{tag}.s1")
        act(lnsc[0:128, 0:F], src_ap, AF.Square, accum=s2, tag=f"{tag}.s2")
        vts(m, s1, 1.0 / F, None, ALU.mult, tag=f"{tag}.m")
        vtt(tt, m, m, ALU.mult, tag=f"{tag}.mm")
        vstt(vv, s2, 1.0 / F, tt, ALU.mult, ALU.subtract, tag=f"{tag}.v")
        act(r, vv, AF.Ln, bias=EPS, tag=f"{tag}.ln")
        act(r, r, AF.Exp, scale=-0.5, tag=f"{tag}.ex")
        if g is None:
            return vts(dst_ap, src_ap, m, r, ALU.subtract, ALU.mult, tag=f"{tag}.ap")
        vts(tmp, src_ap, m, r, ALU.subtract, ALU.mult, tag=f"{tag}.ap")
        vtt(tmp, tmp, g, ALU.mult, tag=f"{tag}.g")
        return vtt(dst_ap, tmp, b, ALU.add, tag=f"{tag}.b")

    for sq in range(NSEQ):
        tok0 = sq * S
        pref = f"s{sq}"

        wzdt = ar.alloc("wzdt", 128, 8 * 2112, BF16)
        wxbc = ar.alloc("wxbc", 128, 8 * CONVD, BF16)
        load_w(wzdt, "W_zdt", D, 2112, tag=f"{pref}wzdt")
        load_w(wxbc, "W_xbc", D, CONVD, tag=f"{pref}wxbc")

        ln0fm = ar.alloc("ln0fm", 128, 8 * 2048, BF16)
        xtile = ar.alloc("xtile", 128, 2 * D, F32)
        ln0t = ar.alloc("ln0t", 128, D, F32)
        dtres = ar.alloc("dtres", 128, NTC * 64, F32)
        fdres = ar.alloc("fdres", 128, NTC * 32, F32)
        zev = ar.alloc("zev", 128, DI, BF16)
        bcf = ar.alloc("bcf", 128, 4 * 2048, BF16)
        stg = ar.alloc("stg", 128, 2 * 516, F32)
        marg = ar.alloc("marg", 128, 60, F32)
        cvo = ar.alloc("cvo", 128, 2 * 512, F32)
        fdsb = ar.alloc("fdsb", 32, 512, F32)
        xctt = ar.alloc("xctt", 128, 2 * 128, BF16)

        for tc in range(NTC):
            xt = xtile[0:128, (tc % 2) * D:(tc % 2) * D + D]
            DMA("sync", xt, hbm(u, D, tok0 + tc * 128, 128), tag=f"{pref}ldx{tc}")
            layernorm(ln0t[0:128, 0:D], xt, D, tag=f"{pref}ln0.{tc}")
            for kb in (0, 4):
                pv = psv(6 if kb == 0 else 7)
                for j in range(4):
                    kc = kb + j
                    trans(pv[0:128, j * 128:j * 128 + 128],
                          ln0t[0:128, kc * 128:kc * 128 + 128], C["IDT32"][:, :],
                          tag=f"{pref}trl{tc}.{kc}")
                for j in range(4):
                    kc = kb + j
                    o = kc * 2048 + tc * 128
                    vcopy(ln0fm[0:128, o:o + 128], pv[0:128, j * 128:j * 128 + 128],
                          tag=f"{pref}evl{tc}.{kc}")

        for tc in range(NTC):
            pz = psv(0, 4)
            pdt = psv(4)
            for kc in range(8):
                lhs = ln0fm[0:128, kc * 2048 + tc * 128: kc * 2048 + tc * 128 + 128]
                for fs in range(4):
                    mm(pz[0:128, fs * 512:fs * 512 + 512], lhs,
                       wzdt[0:128, kc * 2112 + fs * 512: kc * 2112 + fs * 512 + 512],
                       start=(kc == 0), stop=(kc == 7), tag=f"{pref}zmm{tc}")
                mm(pdt[0:128, 0:64], lhs,
                   wzdt[0:128, kc * 2112 + 2048: kc * 2112 + 2112],
                   start=(kc == 0), stop=(kc == 7), tag=f"{pref}dtmm{tc}")
            zv = zev[0:128, 0:DI]
            vcopy(zv, pz[0:128, 0:2048], tag=f"{pref}zev{tc}")
            DMA("gpsimd", hbm(z_sp, DI, tc * 128, 128), zv, tag=f"{pref}zsp{tc}")
            dts = dtres[0:128, tc * 64:tc * 64 + 64]
            vtt(dts, pdt[0:128, 0:64], C["DTB"][:, :], ALU.add, tag=f"{pref}dtb{tc}")
            act(dts, dts, AF.Exp, tag=f"{pref}dte{tc}")
            act(dts, dts, AF.Ln, bias=1.0, tag=f"{pref}dtl{tc}")

        for fc in range(20):
            for kc in range(8):
                sta = wxbc[0:128, kc * CONVD + fc * 128: kc * CONVD + fc * 128 + 128]
                for sp in range(4):
                    mov = ln0fm[0:128, kc * 2048 + sp * 512: kc * 2048 + sp * 512 + 512]
                    mm(psv(sp)[0:128, 0:512], sta, mov,
                       start=(kc == 0), stop=(kc == 7), tag=f"{pref}xbc{fc}.{sp}")
            mg = marg[0:128, fc * 3:fc * 3 + 3]
            memset(mg, 0.0, tag=f"{pref}mg{fc}")
            for sp in range(4):
                base = (sp % 2) * 516
                vcopy(stg[0:128, base + 3:base + 515], psv(sp)[0:128, 0:512],
                      tag=f"{pref}stg{fc}.{sp}")
                vcopy(stg[0:128, base:base + 3], mg, tag=f"{pref}stgm{fc}.{sp}")
                acc = cvo[0:128, (sp % 2) * 512:(sp % 2) * 512 + 512]
                vts(acc, stg[0:128, base:base + 512], C["CWT"][0:128, fc * 4:fc * 4 + 1],
                    None, ALU.mult, tag=f"{pref}cv0.{fc}.{sp}")
                for j in range(1, 4):
                    vstt(acc, stg[0:128, base + j:base + j + 512],
                         C["CWT"][0:128, fc * 4 + j:fc * 4 + j + 1], acc,
                         ALU.mult, ALU.add, tag=f"{pref}cv{j}.{fc}.{sp}")
                vcopy(mg, stg[0:128, base + 512:base + 515], tag=f"{pref}mgu{fc}.{sp}")
                act(acc, acc, AF.Silu, bias=C["CBC"][0:128, fc:fc + 1],
                    tag=f"{pref}silu{fc}.{sp}")
                if fc < 16:
                    mm(psv(4 + sp, P=32)[0:32, 0:512],
                       C["W_fcd"][0:128, fc * 32:fc * 32 + 32], acc,
                       start=(fc == 0), stop=(fc == 15), tag=f"{pref}fd{fc}.{sp}")
                    for tb in range(4):
                        tc = sp * 4 + tb
                        pv = psv(sp)
                        trans(pv[0:128, 0:128],
                              cvo[0:128, (sp % 2) * 512 + tb * 128:(sp % 2) * 512 + tb * 128 + 128],
                              C["IDT32"][:, :], tag=f"{pref}trx{fc}.{tc}")
                        o = (tb % 2) * 128
                        vcopy(xctt[0:128, o:o + 128], pv[0:128, 0:128],
                              tag=f"{pref}evx{fc}.{tc}")
                        DMA("gpsimd",
                            bass.AP(xc_sp.tensor, tc * 128 * DI + fc * 128,
                                    [[DI, 128], [1, 128]]),
                            xctt[0:128, o:o + 128], tag=f"{pref}xcs{fc}.{tc}")
                else:
                    bci = fc - 16
                    vcopy(bcf[0:128, bci * 2048 + sp * 512:bci * 2048 + sp * 512 + 512],
                          acc, tag=f"{pref}bc{fc}.{sp}")
        for sp in range(4):
            vcopy(fdsb[0:32, 0:512], psv(4 + sp, P=32)[0:32, 0:512],
                  tag=f"{pref}fdc{sp}")
            for tb in range(4):
                tc = sp * 4 + tb
                pv = psv(0)
                trans(pv[0:128, 0:32], fdsb[0:32, tb * 128:tb * 128 + 128],
                      C["IDT32"][0:32, 0:32], tag=f"{pref}fdt{tc}")
                vtt(fdres[0:128, tc * 32:tc * 32 + 32], pv[0:128, 0:32],
                    C["DREP"][:, :], ALU.add, tag=f"{pref}fde{tc}")

        ar.free("wzdt", "wxbc", "ln0fm", "xtile", "ln0t", "zev", "stg", "marg",
                "cvo", "fdsb")
        if BUILD_PHASES < 2:
            ar.free("dtres", "fdres", "bcf", "xctt")
            continue

        wout = ar.alloc("wout", 128, 16 * D, BF16)
        load_w(wout, "W_out", DI, D, tag=f"{pref}wout")

        xcld = ar.alloc("xcld", 128, 2 * DI, BF16)
        zld = ar.alloc("zld", 128, 2 * DI, BF16)
        yfwld = ar.alloc("yfwld", 128, 2 * DI, BF16)
        ssd_sc = ar.alloc("ssd_sc", 128, 2 * 256, F32)
        chib = ar.alloc("chib", 32, 2 * 256, BF16)
        bcm = ar.alloc("bcm", 128, 2 * 128, BF16)
        btm = ar.alloc("btm", 128, 2 * 128, BF16)
        xdt = ar.alloc("xdt", 128, 2 * DI, BF16)
        xdtd = ar.alloc("xdtd", 128, 2 * DI, BF16)
        dcl = ar.alloc("dcl", 128, 1024, F32)
        ebuf = ar.alloc("ebuf", 128, 1024, BF16)
        gbuf = ar.alloc("gbuf", 128, 2 * 1024, BF16)
        Sst = ar.alloc("Sst", 128, DI, F32)
        Sb16 = ar.alloc("Sb16", 128, DI, BF16)
        ytl = ar.alloc("ytl", 128, 2 * DI, BF16)
        ysh = ar.alloc("ysh", 128, DI, BF16)
        yoffs = ar.alloc("yoffs", 128, 512, F32)
        ycmb = ar.alloc("ycmb", 128, DI, F32)
        szt = ar.alloc("szt", 128, DI, F32)
        ynfm = ar.alloc("ynfm", 128, 16 * 128, BF16)
        mixt = ar.alloc("mixt", 128, D, F32)
        x1b = ar.alloc("x1b", 128, D, BF16)



        def ssd_chunk(dirn, ci, first):
            fw = dirn == 0
            pa = ci % 2
            dp = f"{pref}{'f' if fw else 'b'}{ci}"
            sso = pa * 256
            apos = ssd_sc[0:128, sso:sso + 32]
            c_sb = ssd_sc[0:128, sso + 32:sso + 64]
            EXPC = ssd_sc[0:128, sso + 64:sso + 96]
            DEC = ssd_sc[0:128, sso + 96:sso + 128]
            LHSDp = LHSDA if pa == 0 else LHSDB
            chio = pa * 256
            dts = dtres.sl(0, 128, ci * 64 + (0 if fw else 32), 32)
            offc = 0 if fw else 256
            offd = 128 if fw else 384
            pm_ = psv(7 if pa == 0 else 6)
            vtt(apos, dts, AEXP[:, :], ALU.mult, tag=f"{dp}.ap")
            mm(pm_[0:128, 0:32], C["TRI"][0:128, offc:offc + 128], apos, True, True,
               tag=f"{dp}.cmm")
            mm(pm_[0:128, 64:96], C["TRI"][0:128, offd:offd + 128], apos, True, True,
               tag=f"{dp}.zmm")
            vcopy(c_sb, pm_[0:128, 0:32], tag=f"{dp}.cev")
            act(EXPC, pm_[0:128, 0:32], AF.Exp, tag=f"{dp}.expc")
            act(DEC, pm_[0:128, 64:96], AF.Exp, tag=f"{dp}.dec")
            trow = 127 if fw else 0
            TOTp = TOTROW[0:1, pa * 32:pa * 32 + 32]
            DMA("sync", TOTp,
                bass.AP(ARh, trow * AR_BF + ssd_sc.off + 2 * (sso + 32),
                        [[AR_BF, 1], [1, 64]]).bitcast(F32),
                tag=f"{dp}.totd")
            act(TOTp, TOTp, AF.Exp, tag=f"{dp}.tote")
            mm(pm_[0:128, 128:160], C["ONEF"][0:1, :], TOTp, True, True,
               tag=f"{dp}.ebmm")
            vcopy(ssd_sc[0:128, sso + 128:sso + 160], pm_[0:128, 128:160],
                  tag=f"{dp}.ebev")
            pT = pm_.sl(0, 32, 192, 128)
            trans(pT, c_sb, C["IDT32"][:, :], tag=f"{dp}.ctr")
            act(chib[0:32, chio:chio + 128], pT, AF.Copy, tag=f"{dp}.chi")
            vtt(chib[0:32, chio + 128:chio + 256], pT, chib[0:32, chio:chio + 128],
                ALU.subtract, tag=f"{dp}.clo")
            act(LHSDp[0:32, :], pT, AF.Copy, scale=-1.0, tag=f"{dp}.nchi")
            vtt(bcm[0:32, pa * 128:pa * 128 + 128], chib[0:32, chio:chio + 128], pT,
                ALU.subtract, tag=f"{dp}.nclo")
            DMA("sync", LHSDp[32:64, :], bcm[0:32, pa * 128:pa * 128 + 128],
                tag=f"{dp}.nlod")
            DMA("sync", RHS_D[64 + 2 * pa:65 + 2 * pa, :],
                chib[0:32, chio:chio + 128], tag=f"{dp}.cfd")
            DMA("sync", RHS_D[65 + 2 * pa:66 + 2 * pa, :],
                chib[0:32, chio + 128:chio + 256], tag=f"{dp}.lfd")
            boff = 0 if fw else 2 * 2048
            Bfm = bcf[0:128, boff + ci * 128: boff + ci * 128 + 128]
            Cfm = bcf[0:128, boff + 2048 + ci * 128: boff + 2048 + ci * 128 + 128]
            mm(pm_[0:128, 320:448], Bfm, Cfm, True, True, tag=f"{dp}.bcmm")
            vtt(bcm[0:128, pa * 128:pa * 128 + 128], pm_[0:128, 320:448],
                C["MASK"][0:128, (0 if fw else 128):(128 if fw else 256)],
                ALU.mult, tag=f"{dp}.bcm")
            trans(pm_.bf(0, 128, 448, 128), Bfm, C["IDT16"][:, :], tag=f"{dp}.btr")
            vcopy(btm[0:128, pa * 128:pa * 128 + 128], pm_.bf(0, 128, 448, 128),
                  tag=f"{dp}.bte")
            vtt(xdt.sl(0, 128, pa * DI, DI, [(64, 32), (1, 64)]),
                xcld.sl(0, 128, pa * DI, DI, [(64, 32), (1, 64)]),
                dtres.sl(0, 128, ci * 64 + (0 if fw else 32), 32, [(1, 32), (0, 64)]),
                ALU.mult, tag=f"{dp}.xdt")
            vtt(xdtd.sl(0, 128, pa * DI, DI, [(64, 32), (1, 64)]),
                xdt.sl(0, 128, pa * DI, DI, [(64, 32), (1, 64)]),
                ssd_sc.sl(0, 128, sso + 96, 32, [(1, 32), (0, 64)]),
                ALU.mult, tag=f"{dp}.xdd")
            if first:
                memset(Sst[0:128, 0:DI], 0.0, tag=f"{dp}.s0")
                memset(Sb16[0:128, 0:DI], 0.0, tag=f"{dp}.sb0")
            else:
                vcopy(Sb16[0:128, 0:DI], Sst[0:128, 0:DI], tag=f"{dp}.sbc")
            yt = ytl[0:128, pa * DI:pa * DI + DI]
            for grp in range(4):
                gp = grp % 2
                pd = psv(0 if gp == 0 else 2, 2)
                for half in range(2):
                    mm(pd[0:128, half * 512:half * 512 + 512], LHSDp[0:68, :],
                       RHS_D[0:68, grp * 1024 + half * 512: grp * 1024 + half * 512 + 512],
                       True, True, tag=f"{dp}.dmm{grp}")
                vts(dcl.sl(0, 128, 0, 1024), pd[0:128, 0:1024], 0.0, None,
                    ALU.min, tag=f"{dp}.dcl{grp}")
                act(ebuf[0:128, 0:1024],
                    dcl.sl(0, 128, 0, 1024), AF.Exp, tag=f"{dp}.exp{grp}")
                vtt(gbuf.sl(0, 128, gp * 1024, 1024, [(128, 8), (1, 128)]),
                    ebuf.sl(0, 128, 0, 1024, [(128, 8), (1, 128)]),
                    bcm.sl(0, 128, pa * 128, 128, [(0, 8), (1, 128)]),
                    ALU.mult, tag=f"{dp}.g{grp}")
                p4 = psv(4)
                p5 = psv(5)
                p6b = psv(6 if pa == 0 else 7)
                pst_bank = p6b
                for j in range(8):
                    h = grp * 8 + j
                    mm(p4[0:128, j * 64:j * 64 + 64],
                       gbuf[0:128, gp * 1024 + j * 128:gp * 1024 + j * 128 + 128],
                       xdt[0:128, pa * DI + h * 64:pa * DI + h * 64 + 64],
                       True, True, tag=f"{dp}.yd{h}")
                    mm(p5[0:128, j * 64:j * 64 + 64], Cfm,
                       Sb16[0:128, h * 64:h * 64 + 64], True, True, tag=f"{dp}.yo{h}")
                    mm(pst_bank[0:128, j * 64:j * 64 + 64],
                       btm[0:128, pa * 128:pa * 128 + 128],
                       xdtd[0:128, pa * DI + h * 64:pa * DI + h * 64 + 64],
                       True, True, tag=f"{dp}.st{h}")
                vtt(yoffs[0:128, 0:512], p5[0:128, 0:512],
                    ssd_sc.sl(0, 128, sso + 64 + grp * 8, 8, [(1, 8), (0, 64)]),
                    ALU.mult, tag=f"{dp}.ye{grp}")
                vtt(ytl.sl(0, 128, pa * DI + grp * 512, 512),
                    yoffs[0:128, 0:512], p4[0:128, 0:512], ALU.add,
                    tag=f"{dp}.ya{grp}")
                vtt(yoffs[0:128, 0:512],
                    xcld.sl(0, 128, pa * DI + grp * 512, 512, [(64, 8), (1, 64)]),
                    bass.AP(C["DREP"], grp * 8, [[32, 128], [1, 8], [0, 64]]),
                    ALU.mult, tag=f"{dp}.yD{grp}")
                vtt(ytl.sl(0, 128, pa * DI + grp * 512, 512),
                    ytl.sl(0, 128, pa * DI + grp * 512, 512), yoffs[0:128, 0:512],
                    ALU.add, tag=f"{dp}.yDa{grp}")
                vtt(Sst.sl(0, 128, grp * 512, 512),
                    Sst.sl(0, 128, grp * 512, 512),
                    ssd_sc.sl(0, 128, sso + 128 + grp * 8, 8, [(1, 8), (0, 64)]),
                    ALU.mult, tag=f"{dp}.sm{grp}")
                vtt(Sst.sl(0, 128, grp * 512, 512),
                    Sst.sl(0, 128, grp * 512, 512), pst_bank[0:128, 0:512],
                    ALU.add, tag=f"{dp}.sa{grp}")
            return yt

        memset(ysh.sl(0, 1, 0, DI), 0.0, tag=f"{pref}y0z")
        DMA("gpsimd", hbm(yfw_sp, DI, 0, 1), ysh.sl(0, 1, 0, DI), tag=f"{pref}y0w")
        DMA("sync", xcld[0:128, 0:DI], hbm(xc_sp, DI, 0, 128), tag=f"{pref}xcl0")
        for ci in range(min(NCH, SSD_LIMIT)):
            if ci + 1 < NCH:
                DMA("sync", xcld[0:128, ((ci + 1) % 2) * DI:((ci + 1) % 2) * DI + DI],
                    hbm(xc_sp, DI, (ci + 1) * 128, 128), tag=f"{pref}xcl{ci+1}")
            yt = ssd_chunk(0, ci, ci == 0)
            DMA("gpsimd", hbm(yfw_sp, DI, ci * 128 + 1, 127),
                ytl.sl(0, 127, (ci % 2) * DI, DI), tag=f"{pref}ysf{ci}")
            if ci + 1 < NCH:
                DMA("gpsimd", hbm(yfw_sp, DI, (ci + 1) * 128, 1),
                    ytl.sl(127, 1, (ci % 2) * DI, DI), tag=f"{pref}ysb{ci}")

        if SSD_LIMIT < NCH:
            ar.free("wout", "xcld", "zld", "yfwld", "ssd_sc", "chib", "bcm", "btm",
                    "xdt", "xdtd", "dcl", "ebuf", "gbuf", "Sst", "Sb16", "ytl",
                    "ysh", "yoffs", "ycmb", "szt", "ynfm", "mixt",
                    "x1b", "dtres", "fdres", "bcf", "xctt")
            continue
        DMA("sync", xcld[0:128, (15 % 2) * DI:(15 % 2) * DI + DI],
            hbm(xc_sp, DI, 15 * 128, 128), tag=f"{pref}bxc15")
        DMA("sync", zld[0:128, (15 % 2) * DI:(15 % 2) * DI + DI],
            hbm(z_sp, DI, 15 * 128, 128), tag=f"{pref}bzl15")
        DMA("sync", yfwld[0:128, (15 % 2) * DI:(15 % 2) * DI + DI],
            hbm(yfw_sp, DI, 15 * 128, 128), tag=f"{pref}byf15")
        for ci in range(NCH - 1, -1, -1):
            if ci > 0:
                nb = ((ci - 1) % 2)
                DMA("sync", xcld[0:128, nb * DI:nb * DI + DI],
                    hbm(xc_sp, DI, (ci - 1) * 128, 128), tag=f"{pref}bxc{ci-1}")
                DMA("sync", zld[0:128, nb * DI:nb * DI + DI],
                    hbm(z_sp, DI, (ci - 1) * 128, 128), tag=f"{pref}bzl{ci-1}")
                DMA("sync", yfwld[0:128, nb * DI:nb * DI + DI],
                    hbm(yfw_sp, DI, (ci - 1) * 128, 128), tag=f"{pref}byf{ci-1}")
            yt = ssd_chunk(1, ci, ci == NCH - 1)
            DMA("sync", ysh.sl(0, 127, 0, DI), ytl.sl(1, 127, (ci % 2) * DI, DI),
                tag=f"{pref}bsh{ci}")
            if ci == NCH - 1:
                memset(ytl.sl(0, 1, ((ci + 1) % 2) * DI, DI), 0.0,
                       tag=f"{pref}bshz")
            DMA("sync", ysh.sl(127, 1, 0, DI),
                ytl.sl(0, 1, ((ci + 1) % 2) * DI, DI), tag=f"{pref}bshc{ci}")
            dp = f"{pref}p4.{ci}"
            cur = (ci % 2)
            vtt(ycmb.sl(0, 128, 0, DI, [(64, 32), (1, 64)]),
                xcld.sl(0, 128, cur * DI, DI, [(64, 32), (1, 64)]),
                fdres.sl(0, 128, ci * 32, 32, [(1, 32), (0, 64)]),
                ALU.mult, tag=f"{dp}.fd")
            vtt(ycmb[0:128, 0:DI], ycmb[0:128, 0:DI],
                yfwld[0:128, cur * DI:cur * DI + DI], ALU.add, tag=f"{dp}.a")
            vtt(ycmb[0:128, 0:DI], ycmb[0:128, 0:DI], ysh[0:128, 0:DI], ALU.add,
                tag=f"{dp}.b")
            act(szt[0:128, 0:DI], zld[0:128, cur * DI:cur * DI + DI], AF.Silu,
                tag=f"{dp}.sz")
            vtt(ycmb[0:128, 0:DI], ycmb[0:128, 0:DI], szt[0:128, 0:DI], ALU.mult,
                tag=f"{dp}.yg")
            act(lnsc[0:128, 0:DI], ycmb[0:128, 0:DI], AF.Square,
                accum=ssd_sc[0:128, 160:161], tag=f"{dp}.ss")
            act(ssd_sc[0:128, 161:162], ssd_sc[0:128, 160:161], AF.Ln,
                bias=EPS, scale=1.0 / DI, tag=f"{dp}.ln")
            act(ssd_sc[0:128, 161:162], ssd_sc[0:128, 161:162], AF.Exp, scale=-0.5,
                tag=f"{dp}.r")
            pm = psv(2, 2)
            for kc in range(16):
                pv = psv(6)
                trans(pv[0:128, 0:128], ycmb[0:128, kc * 128:kc * 128 + 128],
                      C["IDT32"][:, :], tag=f"{dp}.t{kc}")
                act(ynfm[0:128, kc * 128:kc * 128 + 128], pv[0:128, 0:128], AF.Copy,
                    scale=C["NORMW"][0:128, kc:kc + 1], tag=f"{dp}.n{kc}")
            for kc in range(16):
                for half in range(2):
                    mm(pm[0:128, half * 512:half * 512 + 512],
                       ynfm[0:128, kc * 128:kc * 128 + 128],
                       wout[0:128, kc * D + half * 512: kc * D + half * 512 + 512],
                       start=(kc == 0), stop=(kc == 15), tag=f"{dp}.o{kc}")
            act(mixt[0:128, 0:D], pm[0:128, 0:D], AF.Copy,
                scale=ssd_sc[0:128, 161:162], tag=f"{dp}.mx")
            layernorm(mixt[0:128, 0:D], mixt[0:128, 0:D], D, tag=f"{dp}.lo")
            DMA("sync", ycmb[0:128, 0:1024], hbm(u, D, tok0 + ci * 128, 128),
                tag=f"{dp}.lx")
            vtt(mixt[0:128, 0:D], ycmb[0:128, 0:1024], mixt[0:128, 0:D], ALU.add,
                tag=f"{dp}.x1")
            vcopy(x1b[0:128, 0:D], mixt[0:128, 0:D], tag=f"{dp}.x1b")
            DMA("gpsimd", hbm(x1_sp, D, ci * 128, 128), x1b[0:128, 0:D],
                tag=f"{dp}.x1s")

        ar.free("wout", "xcld", "zld", "yfwld", "ssd_sc", "chib", "bcm", "btm",
                "xdt", "xdtd", "dcl", "ebuf", "gbuf", "Sst", "Sb16", "ytl",
                "ysh", "yoffs", "ycmb", "szt", "ynfm", "mixt",
                "x1b", "dtres", "fdres", "bcf", "xctt")
        if BUILD_PHASES < 3:
            continue

        wv = ar.alloc("wv", 128, 8 * D, BF16)
        load_w(wv, "W_v", D, D, tag=f"{pref}wv")
        ln1fm = ar.alloc("ln1fm", 128, 8 * 2048, BF16)
        x1ld = ar.alloc("x1ld", 128, 2 * D, BF16)
        ln1t = ar.alloc("ln1t", 128, D, F32)
        Qb = ar.alloc("Qb", 128, 8 * 2048, BF16)
        Kb = ar.alloc("Kb", 128, 8 * 2048, BF16)
        Vtm = ar.alloc("Vtm", 128, NTC * 1040, BF16)
        MhB = ar.alloc("MhB", 128, 16, F32)
        mhc = ar.alloc("mhc", 16, 4, F32)

        for tc in range(NTC):
            xb = x1ld[0:128, (tc % 2) * D:(tc % 2) * D + D]
            DMA("sync", xb, hbm(x1_sp, D, tc * 128, 128), tag=f"{pref}dx1{tc}")
            layernorm(ln1t[0:128, 0:D], xb, D, tag=f"{pref}ln1.{tc}")
            for kb in (0, 4):
                pv = psv(6 if kb == 0 else 7)
                for j in range(4):
                    kc = kb + j
                    trans(pv[0:128, j * 128:j * 128 + 128],
                          ln1t[0:128, kc * 128:kc * 128 + 128], C["IDT32"][:, :],
                          tag=f"{pref}tl1{tc}.{kc}")
                for j in range(4):
                    kc = kb + j
                    o = kc * 2048 + tc * 128
                    vcopy(ln1fm[0:128, o:o + 128], pv[0:128, j * 128:j * 128 + 128],
                          tag=f"{pref}el1{tc}.{kc}")
        for tc in range(NTC):
            pv = psv(4, 2)
            for half in range(2):
                mm(pv[0:128, half * 512:half * 512 + 512], C["ONEB"][0:1, :],
                   bass.AP(C["BVROW"], half * 512, [[D, 1], [1, 512]]),
                   True, False, tag=f"{pref}vb{tc}")
            for kc in range(8):
                sta = ln1fm[0:128, kc * 2048 + tc * 128: kc * 2048 + tc * 128 + 128]
                for half in range(2):
                    mm(pv[0:128, half * 512:half * 512 + 512], sta,
                       wv[0:128, kc * D + half * 512: kc * D + half * 512 + 512],
                       start=False, stop=(kc == 7), tag=f"{pref}vmm{tc}.{kc}")
            vcopy(Vtm.sl(0, 128, tc * 1040, 1024, [(65, 16), (1, 64)]),
                  pv.sl(0, 128, 0, 1024, [(64, 16), (1, 64)]), tag=f"{pref}vev{tc}")
        memset(Vtm.sl(0, 128, 64, 16 * NTC, [(65, 16 * NTC)]), 1.0,
               tag=f"{pref}vone")
        ar.free("wv")
        for which, dst, biasc in (("q", Qb, C["BQC"]), ("k", Kb, C["BKC"])):
            wbuf = ar.alloc("wqk", 128, 8 * D, BF16)
            load_w(wbuf, "W_" + which, D, D, tag=f"{pref}w{which}")
            for f in range(8):
                for kc in range(8):
                    sta = wbuf[0:128, kc * D + f * 128: kc * D + f * 128 + 128]
                    for ts in range(4):
                        mov = ln1fm[0:128, kc * 2048 + ts * 512: kc * 2048 + ts * 512 + 512]
                        mm(psv(ts)[0:128, 0:512], sta, mov,
                           start=(kc == 0), stop=(kc == 7), tag=f"{pref}{which}mm{f}")
                for ts in range(4):
                    vts(dst[0:128, f * 2048 + ts * 512: f * 2048 + ts * 512 + 512],
                        psv(ts)[0:128, 0:512], biasc[0:128, f:f + 1], None,
                        ALU.add, tag=f"{pref}{which}ev{f}.{ts}")
            ar.free("wqk")
        for which, qsrc in (("q", Qb), ("k", Kb)):
            pq = psv(4, 4, P=16)
            for f in range(8):
                act(lnsc[0:128, 0:2048], qsrc[0:128, f * 2048:f * 2048 + 2048],
                    AF.Square, tag=f"{pref}sq{which}{f}")
                for ts in range(4):
                    mm(pq[0:16, ts * 512:ts * 512 + 512],
                       C["OH"][0:128, f * 16:f * 16 + 16],
                       lnsc.sl(0, 128, ts * 512, 512),
                       start=(f == 0), stop=(f == 7), tag=f"{pref}qn{which}{f}")
            col = 0 if which == "q" else 1
            vred(mhc[0:16, col:col + 1], pq[0:16, 0:2048], ALU.max,
                 tag=f"{pref}mx{which}")
        vts(mhc[0:16, 2:3], mhc[0:16, 0:1], mhc[0:16, 1:2], -1.0 / 16.0,
            ALU.add, ALU.mult, tag=f"{pref}mh")
        DMA("sync", MHFLAT[0:1, 0:16],
            bass.AP(ARh, mhc.off + 4, [[AR_BF, 16], [1, 2]]).bitcast(F32),
            tag=f"{pref}mhd")
        pmh = psv(6)
        mm(pmh[0:128, 0:16], C["ONEF"][0:1, :], MHFLAT[0:1, 0:16], True, True,
           tag=f"{pref}mhb")
        vcopy(MhB[0:128, 0:16], pmh[0:128, 0:16], tag=f"{pref}mhe")
        ar.free("ln1fm", "x1ld", "ln1t", "mhc")
        if BUILD_PHASES < 4:
            ar.free("Qb", "Kb", "Vtm", "MhB")
            continue

        ptb = ar.alloc("ptb", 128, 2 * 2048, BF16)
        ctxu = ar.alloc("ctxu", 66, 2048, BF16)
        ctn = ar.alloc("ctn", 128, 64, BF16)
        ctp = ar.alloc("ctp", 64, 128, BF16)
        rsc = ar.alloc("rsc", 128, 2, F32)
        rs2 = ar.alloc("rs2", 128, 2, BF16)
        for h in range(HA):
            fq = h // 2
            prow = (h % 2) * 64
            pctx = psv(0, 4, P=65)
            for sc in range(NTC):
                sta = Kb[prow:prow + 64, fq * 2048 + sc * 128: fq * 2048 + sc * 128 + 128]
                vsta = Vtm.sl(0, 128, sc * 1040 + h * 65, 65)
                for qh in range(2):
                    pst = psv(4 + 2 * qh, 2)
                    for qs in range(2):
                        q0 = qh * 1024 + qs * 512
                        mm(pst[0:128, qs * 512:qs * 512 + 512], sta,
                           Qb[prow:prow + 64, fq * 2048 + q0: fq * 2048 + q0 + 512],
                           True, True, tag=f"{pref}at{h}.{sc}.{qh}")
                for qh in range(2):
                    pst = psv(4 + 2 * qh, 2)
                    act(ptb.sl(0, 128, qh * 2048, 1024),
                        pst[0:128, 0:1024], AF.Exp,
                        bias=MhB[0:128, h:h + 1], scale=0.125,
                        tag=f"{pref}ax{h}.{sc}.{qh}")
                for qh in range(2):
                    for qs in range(2):
                        q0 = qh * 1024 + qs * 512
                        mm(pctx[0:65, q0:q0 + 512], vsta,
                           ptb.sl(0, 128, qh * 2048 + qs * 512, 512),
                           start=(sc == 0), stop=(sc == NTC - 1),
                           tag=f"{pref}av{h}.{sc}.{qh}")
            vcopy(ctxu[0:65, 0:2048], psv(0, 4, P=65)[0:65, 0:2048],
                  tag=f"{pref}cx{h}")
            for qt in range(NTC):
                p4a = psv(4)
                tb = bass.AP(PSh, 4 * 512, [[4096, 128], [1, 33]]).bitcast(BF16)
                trans(tb, ctxu[0:66, qt * 128:qt * 128 + 128],
                      C["IDT16"][0:66, 0:66], tag=f"{pref}ct{h}.{qt}")
                tb64 = bass.AP(PSh, 4 * 512 + 32, [[4096, 128], [1, 1]]).bitcast(BF16)
                vcopy(rs2[0:128, 0:2], tb64, tag=f"{pref}r2{h}.{qt}")
                vrecip(rsc[0:128, 0:1], rs2[0:128, 0:1], tag=f"{pref}rs{h}.{qt}")
                vts(ctn[0:128, 0:64], p4a.bf(0, 128, 0, 64), rsc[0:128, 0:1], None,
                    ALU.mult, tag=f"{pref}cn{h}.{qt}")
                p5a = psv(5)
                trans(p5a.bf(0, 64, 0, 128), ctn[0:128, 0:64], C["IDT16"][:, :],
                      tag=f"{pref}cr{h}.{qt}")
                vcopy(ctp[0:64, 0:128], p5a.bf(0, 64, 0, 128), tag=f"{pref}ce{h}.{qt}")
                DMA("gpsimd", bass.AP(ctx_sp.tensor,
                                      ((qt * 8 + h // 2) * 128 + prow) * 128,
                                      [[128, 64], [1, 128]]),
                    ctp[0:64, 0:128], tag=f"{pref}cs{h}.{qt}")
        ar.free("ptb", "ctxu", "ctn", "ctp", "rsc", "rs2", "Qb", "Kb", "Vtm", "MhB")
        if BUILD_PHASES < 5:
            continue

        wow = ar.alloc("wow", 128, 8 * D, BF16)
        wgw = ar.alloc("wgw", 128, 8 * 8, BF16)
        load_w(wow, "W_o", D, D, tag=f"{pref}wo")
        load_w(wgw, "W_g", D, 8, tag=f"{pref}wg")
        ctxt = ar.alloc("ctxt", 128, 2 * 1024, BF16)
        attnt = ar.alloc("attnt", 128, D, F32)
        lna = ar.alloc("lna", 128, D, F32)
        lnx = ar.alloc("lnx", 128, D, F32)
        x1ld2 = ar.alloc("x1ld2", 128, 2 * D, BF16)
        ln1t2 = ar.alloc("ln1t2", 128, D, F32)
        x2t = ar.alloc("x2t", 128, D, F32)
        x2b = ar.alloc("x2b", 128, D, BF16)
        x2fm = ar.alloc("x2fm", 128, 8 * 2048, BF16)
        wgb = ar.alloc("wgb", 128, NTC * 8, F32)
        wfm = ar.alloc("wfm", 8, NTC * 128, BF16)
        gsc = ar.alloc("gsc", 128, 16, F32)
        for tc in range(NTC):
            ct = ctxt[0:128, (tc % 2) * 1024:(tc % 2) * 1024 + 1024]
            DMA("sync", ct, bass.AP(ctx_sp.tensor, tc * 8 * 128 * 128,
                                    [[128, 128], [128 * 128, 8], [1, 128]]),
                tag=f"{pref}cl{tc}")
            xb = x1ld2[0:128, (tc % 2) * D:(tc % 2) * D + D]
            DMA("sync", xb, hbm(x1_sp, D, tc * 128, 128), tag=f"{pref}fx1{tc}")
            pw = psv(0, 2)
            for half in range(2):
                mm(pw[0:128, half * 512:half * 512 + 512], C["ONEB"][0:1, :],
                   bass.AP(C["BOROW"], half * 512, [[D, 1], [1, 512]]),
                   True, False, tag=f"{pref}ob{tc}")
            for kc in range(8):
                sta = ctxt[0:128, (tc % 2) * 1024 + kc * 128:(tc % 2) * 1024 + kc * 128 + 128]
                for half in range(2):
                    mm(pw[0:128, half * 512:half * 512 + 512], sta,
                       wow[0:128, kc * D + half * 512: kc * D + half * 512 + 512],
                       start=False, stop=(kc == 7), tag=f"{pref}omm{tc}.{kc}")
            vcopy(attnt[0:128, 0:D], pw[0:128, 0:D], tag=f"{pref}oev{tc}")
            layernorm(lna[0:128, 0:D], attnt[0:128, 0:D], D, tag=f"{pref}la.{tc}",
                      g=C["G1R"][:, :], b=C["BE1R"][:, :], tmp=lnx[0:128, 0:D])
            layernorm(ln1t2[0:128, 0:D], xb, D, tag=f"{pref}l1r.{tc}")
            vtt(x2t[0:128, 0:D], ln1t2[0:128, 0:D], lna[0:128, 0:D], ALU.add,
                tag=f"{pref}x2.{tc}")
            vcopy(x2b[0:128, 0:D], x2t[0:128, 0:D], tag=f"{pref}x2b.{tc}")
            DMA("gpsimd", hbm(x2_sp, D, tc * 128, 128), x2b[0:128, 0:D],
                tag=f"{pref}x2s.{tc}")
            for kb in (0, 4):
                pv = psv(4 if kb == 0 else 5)
                for j in range(4):
                    kc = kb + j
                    trans(pv[0:128, j * 128:j * 128 + 128],
                          x2t[0:128, kc * 128:kc * 128 + 128], C["IDT32"][:, :],
                          tag=f"{pref}tx2{tc}.{kc}")
                for j in range(4):
                    kc = kb + j
                    o = kc * 2048 + tc * 128
                    vcopy(x2fm[0:128, o:o + 128], pv[0:128, j * 128:j * 128 + 128],
                          tag=f"{pref}ex2{tc}.{kc}")
            pg = psv(3)
            for kc in range(8):
                mm(pg[0:128, 0:8],
                   x2fm[0:128, kc * 2048 + tc * 128: kc * 2048 + tc * 128 + 128],
                   wgw[0:128, kc * 8:kc * 8 + 8], start=(kc == 0), stop=(kc == 7),
                   tag=f"{pref}gmm{tc}")
            gl = gsc[0:128, 0:8]
            vtt(gl, pg[0:128, 0:8], C["BGR"][:, :], ALU.add, tag=f"{pref}gb{tc}")
            gm = gsc[0:128, 8:9]
            vred(gm, gl, ALU.max, tag=f"{pref}gm{tc}")
            vts(gl, gl, gm, None, ALU.subtract, tag=f"{pref}gs{tc}")
            act(gl, gl, AF.Exp, tag=f"{pref}ge{tc}")
            vred(gsc[0:128, 9:10], gl, ALU.add, tag=f"{pref}gsum{tc}")
            vrecip(gsc[0:128, 10:11], gsc[0:128, 9:10], tag=f"{pref}gr{tc}")
            vts(gl, gl, gsc[0:128, 10:11], None, ALU.mult, tag=f"{pref}gp{tc}")
            vred(gsc[0:128, 11:12], gl, ALU.max, tag=f"{pref}g1{tc}")
            ismax = wgb[0:128, tc * 8:tc * 8 + 8]
            vts(ismax, gl, gsc[0:128, 11:12], None, ALU.is_ge, tag=f"{pref}im{tc}")
            vstt(ismax, ismax, -1e9, gl, ALU.mult, ALU.add, tag=f"{pref}p2{tc}")
            vred(gsc[0:128, 12:13], ismax, ALU.max, tag=f"{pref}m2{tc}")
            vts(ismax, gl, gsc[0:128, 12:13], None, ALU.is_ge, tag=f"{pref}ge2{tc}")
            vtt(wgb[0:128, tc * 8:tc * 8 + 8], gl, ismax, ALU.mult,
                tag=f"{pref}w{tc}")
            vcopy(lnsc[0:128, 0:8], wgb[0:128, tc * 8:tc * 8 + 8], tag=f"{pref}wc{tc}")
            pgf = psv(3)
            trans(pgf.bf(0, 8, 0, 128), lnsc[0:128, 0:8], C["IDT16"][:, :],
                  tag=f"{pref}wt{tc}")
            vcopy(wfm[0:8, tc * 128:tc * 128 + 128], pgf.bf(0, 8, 0, 128),
                  tag=f"{pref}wf{tc}")
        ar.free("wow", "wgw", "ctxt", "attnt", "lna", "lnx", "x1ld2", "ln1t2",
                "x2t", "x2b")
        if BUILD_PHASES < 6:
            ar.free("x2fm", "wgb", "wfm", "gsc")
            continue

        web = ar.alloc("web", 128, 16384, BF16)
        moeq = ar.alloc("moeq", 128, NTC * 256, F32)
        moebf = ar.alloc("moebf", 128, NTC * D, BF16)
        mst = ar.alloc("mst", 128, 2 * NTC, F32)
        wea = dr["W_e"]
        for fq in range(4):
            stg_ = 0
            for kc in range(8):
                for e_ in range(NE):
                    src = bass.AP(wea.tensor, (e_ * D + kc * 128) * D + fq * 256,
                                  [[D, 128], [1, 256]])
                    DMA("gpsimd", web[0:128, stg_ * 16384 + kc * 2048 + e_ * 256:
                                      stg_ * 16384 + kc * 2048 + e_ * 256 + 256],
                        src, tag=f"{pref}we{fq}.{kc}.{e_}")
            for tc in range(NTC):
                tp_ = (tc % 2) * 256
                pb = psv(0)
                mm(pb[0:128, tp_:tp_ + 256], wfm[0:8, tc * 128:tc * 128 + 128],
                   bass.AP(C["BEXP"], fq * 256, [[D, 8], [1, 256]]), True, True,
                   tag=f"{pref}bx{fq}.{tc}")
                vcopy(moeq[0:128, tc * 256:tc * 256 + 256], pb[0:128, tp_:tp_ + 256],
                      tag=f"{pref}bv{fq}.{tc}")
                for kc in range(8):
                    sta = x2fm[0:128, kc * 2048 + tc * 128: kc * 2048 + tc * 128 + 128]
                    for e_ in range(NE):
                        mm(psv(e_)[0:128, tp_:tp_ + 256],
                           sta,
                           web[0:128, stg_ * 16384 + kc * 2048 + e_ * 256:
                               stg_ * 16384 + kc * 2048 + e_ * 256 + 256],
                           start=(kc == 0), stop=(kc == 7),
                           tag=f"{pref}em{fq}.{tc}.{e_}")
                for e_ in range(NE):
                    vstt(moeq[0:128, tc * 256:tc * 256 + 256],
                         psv(e_)[0:128, tp_:tp_ + 256],
                         wgb[0:128, tc * 8 + e_:tc * 8 + e_ + 1],
                         moeq[0:128, tc * 256:tc * 256 + 256], ALU.mult, ALU.add,
                         tag=f"{pref}ev{fq}.{tc}.{e_}")
            for tc in range(NTC):
                sl_ = moeq[0:128, tc * 256:tc * 256 + 256]
                dst = moebf[0:128, tc * D + fq * 256: tc * D + fq * 256 + 256]
                t1 = lnst[0:128, 6:7]
                t2 = lnst[0:128, 7:8]
                act(dst, sl_, AF.Copy, accum=t1, tag=f"{pref}mc{fq}.{tc}")
                act(lnsc[0:128, 0:256], sl_, AF.Square, accum=t2,
                    tag=f"{pref}ms{fq}.{tc}")
                if fq == 0:
                    vcopy(mst[0:128, tc:tc + 1], t1, tag=f"{pref}m1{fq}.{tc}")
                    vcopy(mst[0:128, NTC + tc:NTC + tc + 1], t2, tag=f"{pref}m2{fq}.{tc}")
                else:
                    vtt(mst[0:128, tc:tc + 1], mst[0:128, tc:tc + 1], t1, ALU.add,
                        tag=f"{pref}m1{fq}.{tc}")
                    vtt(mst[0:128, NTC + tc:NTC + tc + 1],
                        mst[0:128, NTC + tc:NTC + tc + 1], t2, ALU.add,
                        tag=f"{pref}m2{fq}.{tc}")
        ar.free("web", "moeq", "x2fm", "wgb", "wfm", "gsc")
        if BUILD_PHASES < 7:
            ar.free("moebf", "mst")
            continue

        x1ld3 = ar.alloc("x1ld3", 128, 2 * D, BF16)
        x2ld3 = ar.alloc("x2ld3", 128, 2 * D, BF16)
        ffnt = ar.alloc("ffnt", 128, D, F32)
        lnf = ar.alloc("lnf", 128, D, F32)
        outt = ar.alloc("outt", 128, 2 * D, F32)
        for tc in range(NTC):
            xb1 = x1ld3[0:128, (tc % 2) * D:(tc % 2) * D + D]
            xb2 = x2ld3[0:128, (tc % 2) * D:(tc % 2) * D + D]
            DMA("sync", xb1, hbm(x1_sp, D, tc * 128, 128), tag=f"{pref}ix1{tc}")
            DMA("sync", xb2, hbm(x2_sp, D, tc * 128, 128), tag=f"{pref}ix2{tc}")
            dp = f"{pref}fin{tc}"
            m = lnst[0:128, 8:9]
            vv = lnst[0:128, 9:10]
            r = lnst[0:128, 10:11]
            tt = lnst[0:128, 11:12]
            vts(m, mst[0:128, tc:tc + 1], 1.0 / D, None, ALU.mult, tag=f"{dp}.m")
            vtt(tt, m, m, ALU.mult, tag=f"{dp}.mm")
            vstt(vv, mst[0:128, NTC + tc:NTC + tc + 1], 1.0 / D, tt, ALU.mult,
                 ALU.subtract, tag=f"{dp}.v")
            act(r, vv, AF.Ln, bias=EPS, tag=f"{dp}.l")
            act(r, r, AF.Exp, scale=-0.5, tag=f"{dp}.e")
            vts(ffnt[0:128, 0:D], moebf[0:128, tc * D:tc * D + D], m, r,
                ALU.subtract, ALU.mult, tag=f"{dp}.n")
            vtt(ffnt[0:128, 0:D], ffnt[0:128, 0:D], C["G2R"][:, :], ALU.mult,
                tag=f"{dp}.g")
            vtt(ffnt[0:128, 0:D], ffnt[0:128, 0:D], C["BE2R"][:, :], ALU.add,
                tag=f"{dp}.b")
            vtt(ffnt[0:128, 0:D], ffnt[0:128, 0:D], xb2, ALU.add, tag=f"{dp}.x2")
            layernorm(lnf[0:128, 0:D], ffnt[0:128, 0:D], D, tag=f"{dp}.lo")
            ot = outt[0:128, (tc % 2) * D:(tc % 2) * D + D]
            vtt(ot, lnf[0:128, 0:D], xb1, ALU.add, tag=f"{dp}.o")
            DMA("sync", hbm(out, D, tok0 + tc * 128, 128), ot, tag=f"{dp}.st")
        ar.free("x1ld3", "x2ld3", "ffnt", "lnf", "outt", "moebf", "mst")

    sems = {}
    for e in ENGS:
        sems[f"x_{e}"] = stack.enter_context(nc.semaphore(f"x_{e}"))
        for l in range(8):
            sems[f"d_{e}{l}"] = stack.enter_context(nc.semaphore(f"d_{e}{l}"))
    with nc.Block() as blk:
        sch.materialize(nc, blk, sems)
    stack.close()
    return nc


_CACHED = {}


def kernel(**inputs):
    W = host_prep(inputs)
    u_full = np.ascontiguousarray(np.asarray(inputs["L0_attn_u"], np.float32))
    key = "prog"
    if key not in _CACHED:
        _CACHED[key] = build_program()
    nc = _CACHED[key]
    in_maps = []
    for c in range(8):
        m = {k: W[k] for k in HOST_KEYS}
        m["u"] = np.ascontiguousarray(u_full[2 * c:2 * c + 2].reshape(2 * S, D))
        in_maps.append(m)
    res = run_bass_kernel_spmd(nc, in_maps, core_ids=list(range(8)))
    outs = [np.asarray(r["out"], np.float32).reshape(2, S, D) for r in res.results]
    return np.concatenate(outs, axis=0)


# version 07 (final)
# speedup vs baseline: 1.4200x; 1.4200x over previous
import contextlib
import numpy as np
import ml_dtypes
import concourse.bass as bass
import concourse.mybir as mybir
from concourse.bass_utils import run_bass_kernel_spmd

F32 = mybir.dt.float32
BF16 = mybir.dt.bfloat16
AF = mybir.ActivationFunctionType
ALU = mybir.AluOpType
AX = mybir.AxisListType
NPBF = np.dtype(ml_dtypes.bfloat16)

DEBUG_TAPS = False
BUILD_PHASES = 99
NSEQ = 2
SERIAL_DMA = False
SSD_LIMIT = 99

D = 1024
DI = 2048
NH = 32
HD = 64
CONVD = 2560
CK = 128
NCH = 16
S = 2048
EPS = 1e-5
HA = 16
NE = 8
NTC = 16

ENGS = ("tensor", "vector", "scalar", "gpsimd", "sync")


class Op:
    __slots__ = ("eng", "fn", "dma", "idx", "needs_inc", "deps", "tag",
                 "lane", "lane_idx")


class Sched:
    NL = 4

    def __init__(self):
        self.ops = {e: [] for e in ENGS}
        self.wr = {}
        self.rd = {}
        self.psum_names = set()
        self.dma_count = {e: 0 for e in ENGS}
        self.lane_counts = {e: [0] * self.NL for e in ENGS}

    def _rng(self, ap):
        lo = ap.offset
        hi = ap.offset
        for st, ct in ap.ap:
            if st >= 0:
                hi += st * (ct - 1)
            else:
                lo += st * (ct - 1)
        hi += 1
        esz = mybir.dt.size(ap.dtype)
        lo *= esz
        hi *= esz
        name = ap.tensor.name
        if name in self.psum_names:
            lo = (lo // 2048) * 2048
            hi = ((hi + 2047) // 2048) * 2048
        return name, lo, hi

    def add(self, eng, fn, reads=(), writes=(), dma=False, tag=""):
        op = Op()
        op.eng, op.fn, op.dma, op.tag = eng, fn, dma, tag
        op.idx = len(self.ops[eng])
        op.needs_inc = False
        op.lane = -1
        op.lane_idx = 0
        if dma:
            op.lane = self.dma_count[eng] % self.NL
            self.dma_count[eng] += 1
            self.lane_counts[eng][op.lane] += 1
            op.lane_idx = self.lane_counts[eng][op.lane]
        deps = {}

        def want(rec):
            _, _, deng, didx, ddma = rec
            if deng == eng == "tensor" and not ddma and not dma:
                return
            dop = self.ops[deng][didx]
            if ddma:
                key = (deng, dop.lane)
                if deps.get(key, -1) < didx:
                    deps[key] = didx
            else:
                key = (deng, -1)
                if deps.get(key, -1) < didx:
                    deps[key] = didx
                dop.needs_inc = True

        rrs = [self._rng(ap) for ap in reads]
        wrs = [self._rng(ap) for ap in writes]
        for name, lo, hi in rrs:
            for rec in self.wr.get(name, ()):
                if rec[0] < hi and lo < rec[1]:
                    want(rec)
        for name, lo, hi in wrs:
            for rec in self.wr.get(name, ()):
                if rec[0] < hi and lo < rec[1]:
                    want(rec)
            for rec in self.rd.get(name, ()):
                if rec[0] < hi and lo < rec[1]:
                    want(rec)
        for name, lo, hi in rrs:
            self.rd.setdefault(name, []).append((lo, hi, eng, op.idx, dma))
        for name, lo, hi in wrs:
            self.wr.setdefault(name, []).append((lo, hi, eng, op.idx, dma))
        op.deps = deps
        self.ops[eng].append(op)
        return (eng, op.idx)

    def materialize(self, nc, blk, sems):
        NL = self.NL
        exec_sems = {e: sems[f"x_{e}"] for e in ENGS}
        lane_sems = {e: [sems[f"d_{e}{l}"] for l in range(NL)] for e in ENGS}
        exec_pre = {}
        for e in ENGS:
            xp, cx = [], 0
            for op in self.ops[e]:
                if (not op.dma) and op.needs_inc:
                    cx += 1
                xp.append(cx)
            exec_pre[e] = xp
        sched = self

        def body_for(ename):
            def body(engobj):
                waited = {}
                for op in sched.ops[ename]:
                    for (deng, lane), didx in sorted(op.deps.items()):
                        dop = sched.ops[deng][didx]
                        if lane >= 0:
                            val = 16 * dop.lane_idx
                            sem = lane_sems[deng][lane]
                        else:
                            val = exec_pre[deng][didx]
                            sem = exec_sems[deng]
                        k = (deng, lane)
                        if waited.get(k, 0) < val:
                            engobj.wait_ge(sem, val)
                            waited[k] = val
                    if op.dma:
                        if op.lane_idx > 1 and                                 waited.get((ename, op.lane), 0) < 16 * (op.lane_idx - 1):
                            engobj.wait_ge(lane_sems[ename][op.lane],
                                           16 * (op.lane_idx - 1))
                            waited[(ename, op.lane)] = 16 * (op.lane_idx - 1)
                        ins = op.fn(engobj)
                        ins.then_inc(lane_sems[ename][op.lane], 16)
                    else:
                        ins = op.fn(engobj)
                        if op.needs_inc:
                            ins.then_inc(exec_sems[ename], 1)
                for l in range(NL):
                    if sched.lane_counts[ename][l]:
                        engobj.wait_ge(lane_sems[ename][l],
                                       16 * sched.lane_counts[ename][l])
            return body

        blk.tensor(body_for("tensor"))
        blk.vector(body_for("vector"))
        blk.scalar(body_for("scalar"))
        blk.gpsimd(body_for("gpsimd"))
        blk.sync(body_for("sync"))


class View:

    def __init__(self, handle, aw, off, P, F, dtype):
        self.h, self.aw, self.off = handle, aw, off
        self.P, self.F, self.dtype = P, F, dtype
        self.esz = 1 if dtype == BF16 else 2

    def sl(self, p0, pn, f0, fn, fdims=None):
        e = self.esz
        if fdims is None:
            fdims = [(1, fn)]
        if self.dtype == F32:
            st_l, ct_l = fdims[-1]
            if st_l == 1:
                dims = ([[self.aw, pn]]
                        + [[st * e, ct] for st, ct in fdims[:-1]]
                        + [[1, ct_l * e]])
            else:
                assert st_l == 0, "f32 view last dim must be step 0 or 1"
                dims = ([[self.aw, pn]]
                        + [[st * e, ct] for st, ct in fdims[:-1]]
                        + [[0, ct_l], [1, e]])
            a = bass.AP(self.h, p0 * self.aw + self.off + f0 * e, dims)
            return a.bitcast(F32)
        dims = [[self.aw, pn]] + [[st, ct] for st, ct in fdims]
        return bass.AP(self.h, p0 * self.aw + self.off + f0, dims)

    def __getitem__(self, idx):
        p, f = idx
        p0 = p.start or 0
        pn = (p.stop if p.stop is not None else self.P) - p0
        f0 = f.start or 0
        fn = (f.stop if f.stop is not None else self.F) - f0
        return self.sl(p0, pn, f0, fn)


class PView:

    def __init__(self, handle, off, P, F):
        self.h, self.off, self.P, self.F = handle, off, P, F
        self.aw = 4096

    def sl(self, p0, pn, f0, fn, fdims=None):
        if fdims is None:
            fdims = [(1, fn)]
        dims = [[self.aw, pn]] + [[st, ct] for st, ct in fdims]
        return bass.AP(self.h, p0 * self.aw + self.off + f0, dims)

    def __getitem__(self, idx):
        p, f = idx
        p0 = p.start or 0
        pn = (p.stop if p.stop is not None else self.P) - p0
        f0 = f.start or 0
        fn = (f.stop if f.stop is not None else self.F) - f0
        return self.sl(p0, pn, f0, fn)

    def bf(self, p0, pn, f32_off, n_bf):
        assert n_bf % 2 == 0
        a = bass.AP(self.h, p0 * self.aw + self.off + f32_off,
                    [[self.aw, pn], [1, n_bf // 2]])
        return a.bitcast(BF16)


class Arena:
    def __init__(self, handle, total_bf):
        self.h, self.aw = handle, total_bf
        self.live = {}

    def alloc(self, name, P, F, dtype):
        assert name not in self.live
        size = F * (1 if dtype == BF16 else 2)
        off = 0
        for o, s in sorted(self.live.values()):
            if off + size <= o:
                break
            off = max(off, o + s)
        off = (off + 1) // 2 * 2
        assert off + size <= self.aw, f"arena OOM {name}: {off}+{size}>{self.aw}"
        self.live[name] = (off, size)
        return View(self.h, self.aw, off, P, F, dtype)

    def free(self, *names):
        for n in names:
            del self.live[n]


def host_prep(inputs):
    g = {k: np.ascontiguousarray(np.asarray(v, np.float32)) for k, v in inputs.items()}
    W = {}
    ipw = g["L0_attn_in_proj_w"]
    W["W_zdt"] = np.ascontiguousarray(
        np.concatenate([ipw[:DI].T, ipw[DI + CONVD:].T], axis=1))
    W["W_xbc"] = np.ascontiguousarray(ipw[DI:DI + CONVD].T)
    cw = g["L0_attn_conv_w"][:, 0, :]
    W["CWT"] = np.ascontiguousarray(cw.reshape(20, 128, 4).transpose(1, 0, 2).reshape(128, 80))
    W["CBC"] = np.ascontiguousarray(g["L0_attn_conv_b"].reshape(20, 128).T)
    dtb = np.concatenate([g["L0_attn_dt_bias"], g["L0_attn_dt_bias"]])
    W["DTB"] = np.ascontiguousarray(np.broadcast_to(dtb[None, :], (128, 64)))
    W["ALOGR"] = np.ascontiguousarray(np.broadcast_to(g["L0_attn_A_log"][None, :], (128, 32)))
    W["DREP"] = np.ascontiguousarray(np.broadcast_to(g["L0_attn_D"][None, :], (128, 32)))
    W["W_fcd"] = np.ascontiguousarray(
        g["L0_attn_fc_D_w"].T.reshape(16, 128, 32).transpose(1, 0, 2).reshape(128, 512))
    W["NORMW"] = np.ascontiguousarray(g["L0_attn_norm_w"].reshape(16, 128).T)
    W["W_out"] = np.ascontiguousarray(g["L0_attn_out_proj_w"].T)
    W["W_q"] = g["L0_ffn_Wq"]
    W["W_k"] = g["L0_ffn_Wk"]
    W["W_v"] = g["L0_ffn_Wv"]
    W["W_o"] = g["L0_ffn_Wo"]
    W["BQC"] = np.ascontiguousarray(g["L0_ffn_bq"].reshape(8, 128).T)
    W["BKC"] = np.ascontiguousarray(g["L0_ffn_bk"].reshape(8, 128).T)
    W["BVROW"] = np.ascontiguousarray(g["L0_ffn_bv"][None, :].astype(NPBF))
    W["BOROW"] = np.ascontiguousarray(g["L0_ffn_bo"][None, :].astype(NPBF))
    W["W_g"] = g["L0_ffn_Wg"]
    W["BGR"] = np.ascontiguousarray(np.broadcast_to(g["L0_ffn_bg"][None, :], (128, 8)))
    W["W_e"] = g["L0_ffn_We"]
    W["BEXP"] = np.ascontiguousarray(g["L0_ffn_bexp"].astype(NPBF))
    for nm, key in (("G1R", "L0_ffn_g1"), ("BE1R", "L0_ffn_be1"),
                    ("G2R", "L0_ffn_g2"), ("BE2R", "L0_ffn_be2")):
        W[nm] = np.ascontiguousarray(np.broadcast_to(g[key][None, :], (128, D)))

    k = np.arange(128)
    t = np.arange(128)
    TRI = np.zeros((128, 512), np.float32)
    TRI[:, 0:128] = -(k[:, None] <= t[None, :]).astype(np.float32)
    TRI[:, 128:256] = -(k[:, None] > t[None, :]).astype(np.float32)
    TRI[:, 256:384] = -(k[:, None] >= t[None, :]).astype(np.float32)
    TRI[:, 384:512] = -(k[:, None] < t[None, :]).astype(np.float32)
    W["TRI"] = TRI
    MASK = np.zeros((128, 256), np.float32)
    MASK[:, 0:128] = (t[None, :] >= k[:, None]).astype(np.float32)
    MASK[:, 128:256] = (t[None, :] <= k[:, None]).astype(np.float32)
    W["MASK"] = MASK.astype(NPBF)
    W["IDT32"] = np.eye(128, dtype=np.float32)
    W["IDT16"] = np.eye(128, dtype=np.float32).astype(NPBF)
    BD = np.zeros((64, 4096), np.float32)
    for h in range(32):
        BD[h, h * 128:(h + 1) * 128] = 1.0
        BD[32 + h, h * 128:(h + 1) * 128] = 1.0
    W["BDIAG"] = BD.astype(NPBF)
    W["ONEB"] = np.ones((1, 128), NPBF)
    W["ONEF"] = np.ones((1, 128), np.float32)
    OH = np.zeros((128, 128), np.float32)
    for kc in range(8):
        for p in range(128):
            OH[p, kc * 16 + kc * 2 + (p // 64)] = 1.0
    W["OH"] = OH.astype(NPBF)
    W["EPSC"] = np.ascontiguousarray(
        np.stack([np.full(128, EPS, np.float32), np.ones(128, np.float32)], 1))
    return W


HOST_KEYS = ["W_zdt", "W_xbc", "CWT", "CBC", "DTB", "ALOGR", "DREP", "W_fcd",
             "NORMW", "W_out", "W_q", "W_k", "W_v", "W_o", "BQC", "BKC",
             "BVROW", "BOROW", "W_g", "BGR", "W_e", "BEXP", "G1R", "BE1R",
             "G2R", "BE2R", "TRI", "MASK", "IDT32", "IDT16", "BDIAG", "ONEB",
             "ONEF", "OH", "EPSC"]


def build_program():
    nc = bass.Bass("TRN2", target_bir_lowering=False)
    sch = Sched()
    stack = contextlib.ExitStack()

    def din(name, shape, dtype=F32):
        return nc.dram_tensor(name, list(shape), dtype, kind="ExternalInput").ap()

    u = din("u", (2 * S, D))
    dr = {}
    shapes = dict(W_zdt=(D, 2112), W_xbc=(D, CONVD), CWT=(128, 80), CBC=(128, 20),
                  DTB=(128, 64), ALOGR=(128, 32), DREP=(128, 32), W_fcd=(128, 512),
                  NORMW=(128, 16), W_out=(DI, D), W_q=(D, D), W_k=(D, D),
                  W_v=(D, D), W_o=(D, D), BQC=(128, 8), BKC=(128, 8),
                  BVROW=(1, D), BOROW=(1, D), W_g=(D, 8), BGR=(128, 8),
                  W_e=(NE, D, D), BEXP=(NE, D), G1R=(128, D), BE1R=(128, D),
                  G2R=(128, D), BE2R=(128, D), TRI=(128, 512), MASK=(128, 256),
                  IDT32=(128, 128), IDT16=(128, 128), BDIAG=(64, 4096),
                  ONEB=(1, 128), ONEF=(1, 128), OH=(128, 128), EPSC=(128, 2))
    bf16keys = {"MASK", "IDT16", "BDIAG", "ONEB", "OH", "BVROW", "BOROW", "BEXP"}
    for kk in HOST_KEYS:
        dr[kk] = din(kk, shapes[kk], BF16 if kk in bf16keys else F32)

    out = nc.dram_tensor("out", [2 * S, D], F32, kind="ExternalOutput").ap()
    kind_sp = "ExternalOutput" if DEBUG_TAPS else "Internal"
    z_sp = nc.dram_tensor("z_sp", [S, DI], BF16, kind=kind_sp).ap()
    xc_sp = nc.dram_tensor("xc_sp", [S, DI], BF16, kind=kind_sp).ap()
    yfw_sp = nc.dram_tensor("yfw_sp", [S, DI], BF16, kind=kind_sp).ap()
    x1_sp = nc.dram_tensor("x1_sp", [S, D], BF16, kind=kind_sp).ap()
    x2_sp = nc.dram_tensor("x2_sp", [S, D], BF16, kind=kind_sp).ap()
    ctx_sp = nc.dram_tensor("ctx_sp", [D, 2048], BF16, kind=kind_sp).ap()
    dbg_sp = nc.dram_tensor("dbg_sp", [64, 128, 2048], F32, kind=kind_sp).ap() \
        if DEBUG_TAPS else None
    dbg_ct = [0]

    def sbuf(name, shape, dtype):
        return stack.enter_context(nc.sbuf_tensor(name, list(shape), dtype))

    C = {}
    cshape = dict(TRI=((128, 512), F32), MASK=((128, 256), BF16),
                  IDT32=((128, 128), F32), IDT16=((128, 128), BF16),
                  ONEB=((1, 128), BF16), ONEF=((1, 128), F32),
                  OH=((128, 128), BF16), CWT=((128, 80), F32),
                  CBC=((128, 20), F32), DTB=((128, 64), F32),
                  ALOGR=((128, 32), F32), DREP=((128, 32), F32),
                  W_fcd=((128, 512), F32), NORMW=((128, 16), F32),
                  BQC=((128, 8), F32), BKC=((128, 8), F32), BGR=((128, 8), F32),
                  BVROW=((1, D), BF16), BOROW=((1, D), BF16),
                  BEXP=((NE, D), BF16), G1R=((128, D), F32),
                  BE1R=((128, D), F32), G2R=((128, D), F32), BE2R=((128, D), F32),
                  EPSC=((128, 2), F32))
    for kk, (shp, dt_) in cshape.items():
        C[kk] = sbuf("c_" + kk, shp, dt_)
    RHS_D = sbuf("RHS_D", (68, 4096), BF16)
    LHSDA = sbuf("LHSDA", (68, 128), BF16)
    LHSDB = sbuf("LHSDB", (68, 128), BF16)
    TOTROW = sbuf("TOTROW", (1, 64), F32)
    MHFLAT = sbuf("MHFLAT", (1, 16), F32)

    AR_BF = 84 * 1024
    ARh = sbuf("AR", (128, AR_BF), BF16)
    ar = Arena(ARh, AR_BF)
    PSh = stack.enter_context(nc.psum_tensor("PS", [128, 4096], F32))
    sch.psum_names.add(PSh[:, :].tensor.name)

    def psv(bank, nbanks=1, P=128):
        return PView(PSh, bank * 512, P, nbanks * 512)

    def T(fn, reads=(), writes=(), tag=""):
        return sch.add("tensor", fn, reads, writes, tag=tag)

    def V(fn, reads=(), writes=(), tag=""):
        return sch.add("vector", fn, reads, writes, tag=tag)

    def SC(fn, reads=(), writes=(), tag=""):
        return sch.add("scalar", fn, reads, writes, tag=tag)

    def DMA(q, dst, src, tag=""):
        return sch.add(q, lambda e, dd=dst, ss=src: e.dma_start(out=dd, in_=ss),
                       reads=[src], writes=[dst], dma=True, tag=tag)

    def mm(outap, lhs, rhs, start, stop, tag=""):
        return T(lambda e, o=outap, a=lhs, b=rhs, st=start, sp=stop:
                 e.matmul(o, a, b, start=st, stop=sp),
                 reads=[lhs, rhs], writes=[outap], tag=tag)

    def trans(outap, inap, ident, tag=""):
        return T(lambda e, o=outap, i=inap, dd=ident: e.transpose(o, i, dd),
                 reads=[inap, ident], writes=[outap], tag=tag)

    def vcopy(dst, src, tag=""):
        return V(lambda e, dd=dst, ss=src: e.tensor_copy(dd, ss),
                 reads=[src], writes=[dst], tag=tag)

    def vtt(dst, a, b, op, tag=""):
        return V(lambda e, dd=dst, x=a, y=b, o=op: e.tensor_tensor(dd, x, y, o),
                 reads=[a, b], writes=[dst], tag=tag)

    def vts(dst, a, s1, s2, op0, op1=ALU.bypass, tag=""):
        rd = [a] + [x for x in (s1, s2) if isinstance(x, bass.AP)]
        return V(lambda e, dd=dst, x=a, uu=s1, vv=s2, o0=op0, o1=op1:
                 e.tensor_scalar(dd, x, uu, vv, o0, o1),
                 reads=rd, writes=[dst], tag=tag)

    def vstt(dst, a, scal, b, op0, op1, tag=""):
        rd = [a, b] + ([scal] if isinstance(scal, bass.AP) else [])
        return V(lambda e, dd=dst, x=a, s=scal, y=b, o0=op0, o1=op1:
                 e.scalar_tensor_tensor(dd, x, s, y, o0, o1),
                 reads=rd, writes=[dst], tag=tag)

    def vred(dst, src, op, tag=""):
        return V(lambda e, dd=dst, ss=src, o=op: e.tensor_reduce(dd, ss, AX.X, o),
                 reads=[src], writes=[dst], tag=tag)

    def vrecip(dst, src, tag=""):
        return V(lambda e, dd=dst, ss=src: e.reciprocal(dd, ss),
                 reads=[src], writes=[dst], tag=tag)

    nc.const_aps.register(F32, EPS, C["EPSC"][0:128, 0:1])
    nc.const_aps.register(F32, 1.0, C["EPSC"][0:128, 1:2])

    def act(dst, src, func, bias=0.0, scale=1.0, accum=None, tag=""):
        rd = [src] + [x for x in (bias, scale) if isinstance(x, bass.AP)]
        if isinstance(bias, float) and bias != 0.0:
            rd.append(C["EPSC"][0:128, 0:2])
        wr = [dst] + ([accum] if accum is not None else [])
        return SC(lambda e, dd=dst, ss=src, f=func, b=bias, sc=scale, a=accum:
                  e.activation(dd, ss, f, bias=b, scale=sc, accum_out=a),
                  reads=rd, writes=wr, tag=tag)

    def dbg_dump(ap_src, tag=""):
        if dbg_sp is None or dbg_ct[0] >= 64:
            return
        slot = dbg_ct[0]
        dbg_ct[0] += 1
        DMA("sync", bass.AP(dbg_sp.tensor, slot * 128 * 2048,
                            [[2048, 128], [1, ap_src.ap[-1][1]]]), ap_src,
            tag=f"dbg{slot}.{tag}")

    def memset(apx, val, tag=""):
        return V(lambda e, a=apx, v=val: e.memset(a, v),
                 reads=[], writes=[apx], tag=tag)

    def hbm(t_ap, width, row0, nrows, col0=0, ncols=None):
        if ncols is None:
            ncols = width
        return bass.AP(t_ap.tensor, row0 * width + col0, [[width, nrows], [1, ncols]])

    for kk in cshape:
        DMA("sync", C[kk][0:cshape[kk][0][0], 0:cshape[kk][0][1]], dr[kk], tag=f"c_{kk}")
    DMA("sync", RHS_D[0:64, :], dr["BDIAG"], tag="c_bdiag")
    AEXP = C["ALOGR"]
    act(AEXP[:, :], AEXP[:, :], AF.Exp, tag="Aexp")
    memset(bass.AP(LHSDA, 64 * 128, [[128, 4], [1, 128]]), 0.0, tag="lhsdA0")
    memset(bass.AP(LHSDA, 64 * 128, [[128, 2], [1, 128]]), 1.0, tag="lhsdA1")
    memset(bass.AP(LHSDB, 64 * 128, [[128, 4], [1, 128]]), 1.0, tag="lhsdB1")
    memset(bass.AP(LHSDB, 64 * 128, [[128, 2], [1, 128]]), 0.0, tag="lhsdB0")
    memset(RHS_D[64:68, :], 0.0, tag="rhsd0")

    def load_w(view, src_name, rows, cols, tag=""):
        srcap = dr[src_name]
        for kc in range(rows // 128):
            dst = view[0:128, kc * cols:(kc + 1) * cols]
            src = bass.AP(srcap.tensor, srcap.offset + kc * 128 * cols,
                          [[cols, 128], [1, cols]])
            DMA("gpsimd", dst, src, tag=f"{tag}{kc}")

    lnsc = ar.alloc("lnsc", 128, DI, BF16)
    lnst = ar.alloc("lnst", 128, 16, F32)

    def layernorm(dst_ap, src_ap, F, tag="", g=None, b=None, tmp=None):
        s1 = lnst[0:128, 0:1]
        s2 = lnst[0:128, 1:2]
        m = lnst[0:128, 2:3]
        vv = lnst[0:128, 3:4]
        r = lnst[0:128, 4:5]
        tt = lnst[0:128, 5:6]
        vred(s1, src_ap, ALU.add, tag=f"{tag}.s1")
        act(lnsc[0:128, 0:F], src_ap, AF.Square, accum=s2, tag=f"{tag}.s2")
        vts(m, s1, 1.0 / F, None, ALU.mult, tag=f"{tag}.m")
        vtt(tt, m, m, ALU.mult, tag=f"{tag}.mm")
        vstt(vv, s2, 1.0 / F, tt, ALU.mult, ALU.subtract, tag=f"{tag}.v")
        act(r, vv, AF.Ln, bias=EPS, tag=f"{tag}.ln")
        act(r, r, AF.Exp, scale=-0.5, tag=f"{tag}.ex")
        if g is None:
            return vts(dst_ap, src_ap, m, r, ALU.subtract, ALU.mult, tag=f"{tag}.ap")
        vts(tmp, src_ap, m, r, ALU.subtract, ALU.mult, tag=f"{tag}.ap")
        vtt(tmp, tmp, g, ALU.mult, tag=f"{tag}.g")
        return vtt(dst_ap, tmp, b, ALU.add, tag=f"{tag}.b")

    for sq in range(NSEQ):
        tok0 = sq * S
        pref = f"s{sq}"

        wzdt = ar.alloc("wzdt", 128, 8 * 2112, BF16)
        wxbc = ar.alloc("wxbc", 128, 8 * CONVD, BF16)
        load_w(wzdt, "W_zdt", D, 2112, tag=f"{pref}wzdt")
        load_w(wxbc, "W_xbc", D, CONVD, tag=f"{pref}wxbc")

        ln0fm = ar.alloc("ln0fm", 128, 8 * 2048, BF16)
        xtile = ar.alloc("xtile", 128, 2 * D, F32)
        ln0t = ar.alloc("ln0t", 128, D, F32)
        dtres = ar.alloc("dtres", 128, NTC * 64, F32)
        fdres = ar.alloc("fdres", 128, NTC * 32, F32)
        zev = ar.alloc("zev", 128, DI, BF16)
        bcf = ar.alloc("bcf", 128, 4 * 2048, BF16)
        stg = ar.alloc("stg", 128, 2 * 516, F32)
        marg = ar.alloc("marg", 128, 60, F32)
        cvo = ar.alloc("cvo", 128, 2 * 512, F32)
        fdsb = ar.alloc("fdsb", 32, 512, F32)
        xctt = ar.alloc("xctt", 128, 2 * 128, BF16)

        for tc in range(NTC):
            xt = xtile[0:128, (tc % 2) * D:(tc % 2) * D + D]
            DMA("sync", xt, hbm(u, D, tok0 + tc * 128, 128), tag=f"{pref}ldx{tc}")
            layernorm(ln0t[0:128, 0:D], xt, D, tag=f"{pref}ln0.{tc}")
            for kb in (0, 4):
                pv = psv(6 if kb == 0 else 7)
                for j in range(4):
                    kc = kb + j
                    trans(pv[0:128, j * 128:j * 128 + 128],
                          ln0t[0:128, kc * 128:kc * 128 + 128], C["IDT32"][:, :],
                          tag=f"{pref}trl{tc}.{kc}")
                for j in range(4):
                    kc = kb + j
                    o = kc * 2048 + tc * 128
                    vcopy(ln0fm[0:128, o:o + 128], pv[0:128, j * 128:j * 128 + 128],
                          tag=f"{pref}evl{tc}.{kc}")

        for tc in range(NTC):
            pz = psv(0, 4)
            pdt = psv(4)
            for kc in range(8):
                lhs = ln0fm[0:128, kc * 2048 + tc * 128: kc * 2048 + tc * 128 + 128]
                for fs in range(4):
                    mm(pz[0:128, fs * 512:fs * 512 + 512], lhs,
                       wzdt[0:128, kc * 2112 + fs * 512: kc * 2112 + fs * 512 + 512],
                       start=(kc == 0), stop=(kc == 7), tag=f"{pref}zmm{tc}")
                mm(pdt[0:128, 0:64], lhs,
                   wzdt[0:128, kc * 2112 + 2048: kc * 2112 + 2112],
                   start=(kc == 0), stop=(kc == 7), tag=f"{pref}dtmm{tc}")
            zv = zev[0:128, 0:DI]
            vcopy(zv, pz[0:128, 0:2048], tag=f"{pref}zev{tc}")
            DMA("gpsimd", hbm(z_sp, DI, tc * 128, 128), zv, tag=f"{pref}zsp{tc}")
            dts = dtres[0:128, tc * 64:tc * 64 + 64]
            vtt(dts, pdt[0:128, 0:64], C["DTB"][:, :], ALU.add, tag=f"{pref}dtb{tc}")
            act(dts, dts, AF.Exp, tag=f"{pref}dte{tc}")
            act(dts, dts, AF.Ln, bias=1.0, tag=f"{pref}dtl{tc}")

        for fc in range(20):
            for kc in range(8):
                sta = wxbc[0:128, kc * CONVD + fc * 128: kc * CONVD + fc * 128 + 128]
                for sp in range(4):
                    mov = ln0fm[0:128, kc * 2048 + sp * 512: kc * 2048 + sp * 512 + 512]
                    mm(psv(sp)[0:128, 0:512], sta, mov,
                       start=(kc == 0), stop=(kc == 7), tag=f"{pref}xbc{fc}.{sp}")
            mg = marg[0:128, fc * 3:fc * 3 + 3]
            memset(mg, 0.0, tag=f"{pref}mg{fc}")
            for sp in range(4):
                base = (sp % 2) * 516
                vcopy(stg[0:128, base + 3:base + 515], psv(sp)[0:128, 0:512],
                      tag=f"{pref}stg{fc}.{sp}")
                vcopy(stg[0:128, base:base + 3], mg, tag=f"{pref}stgm{fc}.{sp}")
                acc = cvo[0:128, (sp % 2) * 512:(sp % 2) * 512 + 512]
                vts(acc, stg[0:128, base:base + 512], C["CWT"][0:128, fc * 4:fc * 4 + 1],
                    None, ALU.mult, tag=f"{pref}cv0.{fc}.{sp}")
                for j in range(1, 4):
                    vstt(acc, stg[0:128, base + j:base + j + 512],
                         C["CWT"][0:128, fc * 4 + j:fc * 4 + j + 1], acc,
                         ALU.mult, ALU.add, tag=f"{pref}cv{j}.{fc}.{sp}")
                vcopy(mg, stg[0:128, base + 512:base + 515], tag=f"{pref}mgu{fc}.{sp}")
                act(acc, acc, AF.Silu, bias=C["CBC"][0:128, fc:fc + 1],
                    tag=f"{pref}silu{fc}.{sp}")
                if fc < 16:
                    mm(psv(4 + sp, P=32)[0:32, 0:512],
                       C["W_fcd"][0:128, fc * 32:fc * 32 + 32], acc,
                       start=(fc == 0), stop=(fc == 15), tag=f"{pref}fd{fc}.{sp}")
                    for tb in range(4):
                        tc = sp * 4 + tb
                        pv = psv(sp)
                        trans(pv[0:128, 0:128],
                              cvo[0:128, (sp % 2) * 512 + tb * 128:(sp % 2) * 512 + tb * 128 + 128],
                              C["IDT32"][:, :], tag=f"{pref}trx{fc}.{tc}")
                        o = (tb % 2) * 128
                        vcopy(xctt[0:128, o:o + 128], pv[0:128, 0:128],
                              tag=f"{pref}evx{fc}.{tc}")
                        DMA("gpsimd",
                            bass.AP(xc_sp.tensor, tc * 128 * DI + fc * 128,
                                    [[DI, 128], [1, 128]]),
                            xctt[0:128, o:o + 128], tag=f"{pref}xcs{fc}.{tc}")
                else:
                    bci = fc - 16
                    vcopy(bcf[0:128, bci * 2048 + sp * 512:bci * 2048 + sp * 512 + 512],
                          acc, tag=f"{pref}bc{fc}.{sp}")
        for sp in range(4):
            vcopy(fdsb[0:32, 0:512], psv(4 + sp, P=32)[0:32, 0:512],
                  tag=f"{pref}fdc{sp}")
            for tb in range(4):
                tc = sp * 4 + tb
                pv = psv(0)
                trans(pv[0:128, 0:32], fdsb[0:32, tb * 128:tb * 128 + 128],
                      C["IDT32"][0:32, 0:32], tag=f"{pref}fdt{tc}")
                vtt(fdres[0:128, tc * 32:tc * 32 + 32], pv[0:128, 0:32],
                    C["DREP"][:, :], ALU.add, tag=f"{pref}fde{tc}")

        ar.free("wzdt", "wxbc", "ln0fm", "xtile", "ln0t", "zev", "stg", "marg",
                "cvo", "fdsb")
        if BUILD_PHASES < 2:
            ar.free("dtres", "fdres", "bcf", "xctt")
            continue

        wout = ar.alloc("wout", 128, 16 * D, BF16)
        load_w(wout, "W_out", DI, D, tag=f"{pref}wout")

        xcld = ar.alloc("xcld", 128, 2 * DI, BF16)
        zld = ar.alloc("zld", 128, 2 * DI, BF16)
        yfwld = ar.alloc("yfwld", 128, 2 * DI, BF16)
        ssd_sc = ar.alloc("ssd_sc", 128, 2 * 256, F32)
        chib = ar.alloc("chib", 32, 2 * 256, BF16)
        bcm = ar.alloc("bcm", 128, 2 * 128, BF16)
        btm = ar.alloc("btm", 128, 2 * 128, BF16)
        xdt = ar.alloc("xdt", 128, 2 * DI, BF16)
        xdtd = ar.alloc("xdtd", 128, 2 * DI, BF16)
        dcl = ar.alloc("dcl", 128, 1024, F32)
        ebuf = ar.alloc("ebuf", 128, 1024, BF16)
        gbuf = ar.alloc("gbuf", 128, 2 * 1024, BF16)
        Sst = ar.alloc("Sst", 128, DI, F32)
        Sb16 = ar.alloc("Sb16", 128, DI, BF16)
        ytl = ar.alloc("ytl", 128, 2 * DI, BF16)
        ysh = ar.alloc("ysh", 128, DI, BF16)
        yoffs = ar.alloc("yoffs", 128, 512, F32)
        ycmb = ar.alloc("ycmb", 128, DI, F32)
        szt = ar.alloc("szt", 128, DI, F32)
        ynfm = ar.alloc("ynfm", 128, 16 * 128, BF16)
        mixt = ar.alloc("mixt", 128, D, F32)
        x1b = ar.alloc("x1b", 128, D, BF16)



        def ssd_chunk(dirn, ci, first):
            fw = dirn == 0
            pa = ci % 2
            dp = f"{pref}{'f' if fw else 'b'}{ci}"
            sso = pa * 256
            apos = ssd_sc[0:128, sso:sso + 32]
            c_sb = ssd_sc[0:128, sso + 32:sso + 64]
            EXPC = ssd_sc[0:128, sso + 64:sso + 96]
            DEC = ssd_sc[0:128, sso + 96:sso + 128]
            LHSDp = LHSDA if pa == 0 else LHSDB
            chio = pa * 256
            dts = dtres.sl(0, 128, ci * 64 + (0 if fw else 32), 32)
            offc = 0 if fw else 256
            offd = 128 if fw else 384
            pm_ = psv(7 if pa == 0 else 6)
            vtt(apos, dts, AEXP[:, :], ALU.mult, tag=f"{dp}.ap")
            mm(pm_[0:128, 0:32], C["TRI"][0:128, offc:offc + 128], apos, True, True,
               tag=f"{dp}.cmm")
            mm(pm_[0:128, 64:96], C["TRI"][0:128, offd:offd + 128], apos, True, True,
               tag=f"{dp}.zmm")
            vcopy(c_sb, pm_[0:128, 0:32], tag=f"{dp}.cev")
            act(EXPC, pm_[0:128, 0:32], AF.Exp, tag=f"{dp}.expc")
            act(DEC, pm_[0:128, 64:96], AF.Exp, tag=f"{dp}.dec")
            trow = 127 if fw else 0
            TOTp = TOTROW[0:1, pa * 32:pa * 32 + 32]
            DMA("sync", TOTp,
                bass.AP(ARh, trow * AR_BF + ssd_sc.off + 2 * (sso + 32),
                        [[AR_BF, 1], [1, 64]]).bitcast(F32),
                tag=f"{dp}.totd")
            act(TOTp, TOTp, AF.Exp, tag=f"{dp}.tote")
            mm(pm_[0:128, 128:160], C["ONEF"][0:1, :], TOTp, True, True,
               tag=f"{dp}.ebmm")
            vcopy(ssd_sc[0:128, sso + 128:sso + 160], pm_[0:128, 128:160],
                  tag=f"{dp}.ebev")
            pT = pm_.sl(0, 32, 192, 128)
            trans(pT, c_sb, C["IDT32"][:, :], tag=f"{dp}.ctr")
            act(chib[0:32, chio:chio + 128], pT, AF.Copy, tag=f"{dp}.chi")
            vtt(chib[0:32, chio + 128:chio + 256], pT, chib[0:32, chio:chio + 128],
                ALU.subtract, tag=f"{dp}.clo")
            act(LHSDp[0:32, :], pT, AF.Copy, scale=-1.0, tag=f"{dp}.nchi")
            vtt(bcm[0:32, pa * 128:pa * 128 + 128], chib[0:32, chio:chio + 128], pT,
                ALU.subtract, tag=f"{dp}.nclo")
            DMA("sync", LHSDp[32:64, :], bcm[0:32, pa * 128:pa * 128 + 128],
                tag=f"{dp}.nlod")
            DMA("sync", RHS_D[64 + 2 * pa:65 + 2 * pa, :],
                chib[0:32, chio:chio + 128], tag=f"{dp}.cfd")
            DMA("sync", RHS_D[65 + 2 * pa:66 + 2 * pa, :],
                chib[0:32, chio + 128:chio + 256], tag=f"{dp}.lfd")
            boff = 0 if fw else 2 * 2048
            Bfm = bcf[0:128, boff + ci * 128: boff + ci * 128 + 128]
            Cfm = bcf[0:128, boff + 2048 + ci * 128: boff + 2048 + ci * 128 + 128]
            mm(pm_[0:128, 320:448], Bfm, Cfm, True, True, tag=f"{dp}.bcmm")
            vtt(bcm[0:128, pa * 128:pa * 128 + 128], pm_[0:128, 320:448],
                C["MASK"][0:128, (0 if fw else 128):(128 if fw else 256)],
                ALU.mult, tag=f"{dp}.bcm")
            trans(pm_.bf(0, 128, 448, 128), Bfm, C["IDT16"][:, :], tag=f"{dp}.btr")
            vcopy(btm[0:128, pa * 128:pa * 128 + 128], pm_.bf(0, 128, 448, 128),
                  tag=f"{dp}.bte")
            vtt(xdt.sl(0, 128, pa * DI, DI, [(64, 32), (1, 64)]),
                xcld.sl(0, 128, pa * DI, DI, [(64, 32), (1, 64)]),
                dtres.sl(0, 128, ci * 64 + (0 if fw else 32), 32, [(1, 32), (0, 64)]),
                ALU.mult, tag=f"{dp}.xdt")
            vtt(xdtd.sl(0, 128, pa * DI, DI, [(64, 32), (1, 64)]),
                xdt.sl(0, 128, pa * DI, DI, [(64, 32), (1, 64)]),
                ssd_sc.sl(0, 128, sso + 96, 32, [(1, 32), (0, 64)]),
                ALU.mult, tag=f"{dp}.xdd")
            if first:
                memset(Sst[0:128, 0:DI], 0.0, tag=f"{dp}.s0")
                memset(Sb16[0:128, 0:DI], 0.0, tag=f"{dp}.sb0")
            else:
                vcopy(Sb16[0:128, 0:DI], Sst[0:128, 0:DI], tag=f"{dp}.sbc")
            yt = ytl[0:128, pa * DI:pa * DI + DI]
            for grp in range(4):
                gp = grp % 2
                pd = psv(0 if gp == 0 else 2, 2)
                for half in range(2):
                    mm(pd[0:128, half * 512:half * 512 + 512], LHSDp[0:68, :],
                       RHS_D[0:68, grp * 1024 + half * 512: grp * 1024 + half * 512 + 512],
                       True, True, tag=f"{dp}.dmm{grp}")
                vts(dcl.sl(0, 128, 0, 1024), pd[0:128, 0:1024], 0.0, None,
                    ALU.min, tag=f"{dp}.dcl{grp}")
                act(ebuf[0:128, 0:1024],
                    dcl.sl(0, 128, 0, 1024), AF.Exp, tag=f"{dp}.exp{grp}")
                vtt(gbuf.sl(0, 128, gp * 1024, 1024, [(128, 8), (1, 128)]),
                    ebuf.sl(0, 128, 0, 1024, [(128, 8), (1, 128)]),
                    bcm.sl(0, 128, pa * 128, 128, [(0, 8), (1, 128)]),
                    ALU.mult, tag=f"{dp}.g{grp}")
                p4 = psv(4)
                p5 = psv(5)
                p6b = psv(6 if pa == 0 else 7)
                pst_bank = p6b
                mm(p5[0:128, 0:512], Cfm,
                   Sb16[0:128, grp * 512:grp * 512 + 512], True, True,
                   tag=f"{dp}.yo{grp}")
                mm(pst_bank[0:128, 0:512],
                   btm[0:128, pa * 128:pa * 128 + 128],
                   xdtd[0:128, pa * DI + grp * 512:pa * DI + grp * 512 + 512],
                   True, True, tag=f"{dp}.st{grp}")
                for j in range(8):
                    h = grp * 8 + j
                    mm(p4[0:128, j * 64:j * 64 + 64],
                       gbuf[0:128, gp * 1024 + j * 128:gp * 1024 + j * 128 + 128],
                       xdt[0:128, pa * DI + h * 64:pa * DI + h * 64 + 64],
                       True, True, tag=f"{dp}.yd{h}")
                vtt(yoffs[0:128, 0:512], p5[0:128, 0:512],
                    ssd_sc.sl(0, 128, sso + 64 + grp * 8, 8, [(1, 8), (0, 64)]),
                    ALU.mult, tag=f"{dp}.ye{grp}")
                vtt(ytl.sl(0, 128, pa * DI + grp * 512, 512),
                    yoffs[0:128, 0:512], p4[0:128, 0:512], ALU.add,
                    tag=f"{dp}.ya{grp}")
                vtt(yoffs[0:128, 0:512],
                    xcld.sl(0, 128, pa * DI + grp * 512, 512, [(64, 8), (1, 64)]),
                    bass.AP(C["DREP"], grp * 8, [[32, 128], [1, 8], [0, 64]]),
                    ALU.mult, tag=f"{dp}.yD{grp}")
                vtt(ytl.sl(0, 128, pa * DI + grp * 512, 512),
                    ytl.sl(0, 128, pa * DI + grp * 512, 512), yoffs[0:128, 0:512],
                    ALU.add, tag=f"{dp}.yDa{grp}")
                vtt(Sst.sl(0, 128, grp * 512, 512),
                    Sst.sl(0, 128, grp * 512, 512),
                    ssd_sc.sl(0, 128, sso + 128 + grp * 8, 8, [(1, 8), (0, 64)]),
                    ALU.mult, tag=f"{dp}.sm{grp}")
                vtt(Sst.sl(0, 128, grp * 512, 512),
                    Sst.sl(0, 128, grp * 512, 512), pst_bank[0:128, 0:512],
                    ALU.add, tag=f"{dp}.sa{grp}")
            return yt

        memset(ysh.sl(0, 1, 0, DI), 0.0, tag=f"{pref}y0z")
        DMA("gpsimd", hbm(yfw_sp, DI, 0, 1), ysh.sl(0, 1, 0, DI), tag=f"{pref}y0w")
        DMA("sync", xcld[0:128, 0:DI], hbm(xc_sp, DI, 0, 128), tag=f"{pref}xcl0")
        for ci in range(min(NCH, SSD_LIMIT)):
            if ci + 1 < NCH:
                DMA("sync", xcld[0:128, ((ci + 1) % 2) * DI:((ci + 1) % 2) * DI + DI],
                    hbm(xc_sp, DI, (ci + 1) * 128, 128), tag=f"{pref}xcl{ci+1}")
            yt = ssd_chunk(0, ci, ci == 0)
            DMA("gpsimd", hbm(yfw_sp, DI, ci * 128 + 1, 127),
                ytl.sl(0, 127, (ci % 2) * DI, DI), tag=f"{pref}ysf{ci}")
            if ci + 1 < NCH:
                DMA("gpsimd", hbm(yfw_sp, DI, (ci + 1) * 128, 1),
                    ytl.sl(127, 1, (ci % 2) * DI, DI), tag=f"{pref}ysb{ci}")

        if SSD_LIMIT < NCH:
            ar.free("wout", "xcld", "zld", "yfwld", "ssd_sc", "chib", "bcm", "btm",
                    "xdt", "xdtd", "dcl", "ebuf", "gbuf", "Sst", "Sb16", "ytl",
                    "ysh", "yoffs", "ycmb", "szt", "ynfm", "mixt",
                    "x1b", "dtres", "fdres", "bcf", "xctt")
            continue
        DMA("sync", xcld[0:128, (15 % 2) * DI:(15 % 2) * DI + DI],
            hbm(xc_sp, DI, 15 * 128, 128), tag=f"{pref}bxc15")
        DMA("sync", zld[0:128, (15 % 2) * DI:(15 % 2) * DI + DI],
            hbm(z_sp, DI, 15 * 128, 128), tag=f"{pref}bzl15")
        DMA("sync", yfwld[0:128, (15 % 2) * DI:(15 % 2) * DI + DI],
            hbm(yfw_sp, DI, 15 * 128, 128), tag=f"{pref}byf15")
        for ci in range(NCH - 1, -1, -1):
            if ci > 0:
                nb = ((ci - 1) % 2)
                DMA("sync", xcld[0:128, nb * DI:nb * DI + DI],
                    hbm(xc_sp, DI, (ci - 1) * 128, 128), tag=f"{pref}bxc{ci-1}")
                DMA("sync", zld[0:128, nb * DI:nb * DI + DI],
                    hbm(z_sp, DI, (ci - 1) * 128, 128), tag=f"{pref}bzl{ci-1}")
                DMA("sync", yfwld[0:128, nb * DI:nb * DI + DI],
                    hbm(yfw_sp, DI, (ci - 1) * 128, 128), tag=f"{pref}byf{ci-1}")
            yt = ssd_chunk(1, ci, ci == NCH - 1)
            DMA("sync", ysh.sl(0, 127, 0, DI), ytl.sl(1, 127, (ci % 2) * DI, DI),
                tag=f"{pref}bsh{ci}")
            if ci == NCH - 1:
                memset(ytl.sl(0, 1, ((ci + 1) % 2) * DI, DI), 0.0,
                       tag=f"{pref}bshz")
            DMA("sync", ysh.sl(127, 1, 0, DI),
                ytl.sl(0, 1, ((ci + 1) % 2) * DI, DI), tag=f"{pref}bshc{ci}")
            dp = f"{pref}p4.{ci}"
            cur = (ci % 2)
            vtt(ycmb.sl(0, 128, 0, DI, [(64, 32), (1, 64)]),
                xcld.sl(0, 128, cur * DI, DI, [(64, 32), (1, 64)]),
                fdres.sl(0, 128, ci * 32, 32, [(1, 32), (0, 64)]),
                ALU.mult, tag=f"{dp}.fd")
            vtt(ycmb[0:128, 0:DI], ycmb[0:128, 0:DI],
                yfwld[0:128, cur * DI:cur * DI + DI], ALU.add, tag=f"{dp}.a")
            vtt(ycmb[0:128, 0:DI], ycmb[0:128, 0:DI], ysh[0:128, 0:DI], ALU.add,
                tag=f"{dp}.b")
            act(szt[0:128, 0:DI], zld[0:128, cur * DI:cur * DI + DI], AF.Silu,
                tag=f"{dp}.sz")
            vtt(ycmb[0:128, 0:DI], ycmb[0:128, 0:DI], szt[0:128, 0:DI], ALU.mult,
                tag=f"{dp}.yg")
            act(lnsc[0:128, 0:DI], ycmb[0:128, 0:DI], AF.Square,
                accum=ssd_sc[0:128, 160:161], tag=f"{dp}.ss")
            act(ssd_sc[0:128, 161:162], ssd_sc[0:128, 160:161], AF.Ln,
                bias=EPS, scale=1.0 / DI, tag=f"{dp}.ln")
            act(ssd_sc[0:128, 161:162], ssd_sc[0:128, 161:162], AF.Exp, scale=-0.5,
                tag=f"{dp}.r")
            pm = psv(2, 2)
            for kc in range(16):
                pv = psv(6)
                trans(pv[0:128, 0:128], ycmb[0:128, kc * 128:kc * 128 + 128],
                      C["IDT32"][:, :], tag=f"{dp}.t{kc}")
                act(ynfm[0:128, kc * 128:kc * 128 + 128], pv[0:128, 0:128], AF.Copy,
                    scale=C["NORMW"][0:128, kc:kc + 1], tag=f"{dp}.n{kc}")
            for kc in range(16):
                for half in range(2):
                    mm(pm[0:128, half * 512:half * 512 + 512],
                       ynfm[0:128, kc * 128:kc * 128 + 128],
                       wout[0:128, kc * D + half * 512: kc * D + half * 512 + 512],
                       start=(kc == 0), stop=(kc == 15), tag=f"{dp}.o{kc}")
            act(mixt[0:128, 0:D], pm[0:128, 0:D], AF.Copy,
                scale=ssd_sc[0:128, 161:162], tag=f"{dp}.mx")
            layernorm(mixt[0:128, 0:D], mixt[0:128, 0:D], D, tag=f"{dp}.lo")
            DMA("sync", ycmb[0:128, 0:1024], hbm(u, D, tok0 + ci * 128, 128),
                tag=f"{dp}.lx")
            vtt(mixt[0:128, 0:D], ycmb[0:128, 0:1024], mixt[0:128, 0:D], ALU.add,
                tag=f"{dp}.x1")
            vcopy(x1b[0:128, 0:D], mixt[0:128, 0:D], tag=f"{dp}.x1b")
            DMA("gpsimd", hbm(x1_sp, D, ci * 128, 128), x1b[0:128, 0:D],
                tag=f"{dp}.x1s")

        ar.free("wout", "xcld", "zld", "yfwld", "ssd_sc", "chib", "bcm", "btm",
                "xdt", "xdtd", "dcl", "ebuf", "gbuf", "Sst", "Sb16", "ytl",
                "ysh", "yoffs", "ycmb", "szt", "ynfm", "mixt",
                "x1b", "dtres", "fdres", "bcf", "xctt")
        if BUILD_PHASES < 3:
            continue

        wv = ar.alloc("wv", 128, 8 * D, BF16)
        load_w(wv, "W_v", D, D, tag=f"{pref}wv")
        ln1fm = ar.alloc("ln1fm", 128, 8 * 2048, BF16)
        x1ld = ar.alloc("x1ld", 128, 2 * D, BF16)
        ln1t = ar.alloc("ln1t", 128, D, F32)
        Qb = ar.alloc("Qb", 128, 8 * 2048, BF16)
        Kb = ar.alloc("Kb", 128, 8 * 2048, BF16)
        Vtm = ar.alloc("Vtm", 128, NTC * 1040, BF16)
        MhB = ar.alloc("MhB", 128, 16, F32)
        mhc = ar.alloc("mhc", 16, 4, F32)

        for tc in range(NTC):
            xb = x1ld[0:128, (tc % 2) * D:(tc % 2) * D + D]
            DMA("sync", xb, hbm(x1_sp, D, tc * 128, 128), tag=f"{pref}dx1{tc}")
            layernorm(ln1t[0:128, 0:D], xb, D, tag=f"{pref}ln1.{tc}")
            for kb in (0, 4):
                pv = psv(6 if kb == 0 else 7)
                for j in range(4):
                    kc = kb + j
                    trans(pv[0:128, j * 128:j * 128 + 128],
                          ln1t[0:128, kc * 128:kc * 128 + 128], C["IDT32"][:, :],
                          tag=f"{pref}tl1{tc}.{kc}")
                for j in range(4):
                    kc = kb + j
                    o = kc * 2048 + tc * 128
                    vcopy(ln1fm[0:128, o:o + 128], pv[0:128, j * 128:j * 128 + 128],
                          tag=f"{pref}el1{tc}.{kc}")
        for tc in range(NTC):
            pv = psv(4, 2)
            for half in range(2):
                mm(pv[0:128, half * 512:half * 512 + 512], C["ONEB"][0:1, :],
                   bass.AP(C["BVROW"], half * 512, [[D, 1], [1, 512]]),
                   True, False, tag=f"{pref}vb{tc}")
            for kc in range(8):
                sta = ln1fm[0:128, kc * 2048 + tc * 128: kc * 2048 + tc * 128 + 128]
                for half in range(2):
                    mm(pv[0:128, half * 512:half * 512 + 512], sta,
                       wv[0:128, kc * D + half * 512: kc * D + half * 512 + 512],
                       start=False, stop=(kc == 7), tag=f"{pref}vmm{tc}.{kc}")
            vcopy(Vtm.sl(0, 128, tc * 1040, 1024, [(65, 16), (1, 64)]),
                  pv.sl(0, 128, 0, 1024, [(64, 16), (1, 64)]), tag=f"{pref}vev{tc}")
        memset(Vtm.sl(0, 128, 64, 16 * NTC, [(65, 16 * NTC)]), 1.0,
               tag=f"{pref}vone")
        ar.free("wv")
        for which, dst, biasc in (("q", Qb, C["BQC"]), ("k", Kb, C["BKC"])):
            wbuf = ar.alloc("wqk", 128, 8 * D, BF16)
            load_w(wbuf, "W_" + which, D, D, tag=f"{pref}w{which}")
            for f in range(8):
                for kc in range(8):
                    sta = wbuf[0:128, kc * D + f * 128: kc * D + f * 128 + 128]
                    for ts in range(4):
                        mov = ln1fm[0:128, kc * 2048 + ts * 512: kc * 2048 + ts * 512 + 512]
                        mm(psv(ts)[0:128, 0:512], sta, mov,
                           start=(kc == 0), stop=(kc == 7), tag=f"{pref}{which}mm{f}")
                for ts in range(4):
                    vts(dst[0:128, f * 2048 + ts * 512: f * 2048 + ts * 512 + 512],
                        psv(ts)[0:128, 0:512], biasc[0:128, f:f + 1], None,
                        ALU.add, tag=f"{pref}{which}ev{f}.{ts}")
            ar.free("wqk")
        for which, qsrc in (("q", Qb), ("k", Kb)):
            pq = psv(4, 4, P=16)
            for f in range(8):
                act(lnsc[0:128, 0:2048], qsrc[0:128, f * 2048:f * 2048 + 2048],
                    AF.Square, tag=f"{pref}sq{which}{f}")
                for ts in range(4):
                    mm(pq[0:16, ts * 512:ts * 512 + 512],
                       C["OH"][0:128, f * 16:f * 16 + 16],
                       lnsc.sl(0, 128, ts * 512, 512),
                       start=(f == 0), stop=(f == 7), tag=f"{pref}qn{which}{f}")
            col = 0 if which == "q" else 1
            vred(mhc[0:16, col:col + 1], pq[0:16, 0:2048], ALU.max,
                 tag=f"{pref}mx{which}")
        vts(mhc[0:16, 2:3], mhc[0:16, 0:1], mhc[0:16, 1:2], -1.0 / 16.0,
            ALU.add, ALU.mult, tag=f"{pref}mh")
        DMA("sync", MHFLAT[0:1, 0:16],
            bass.AP(ARh, mhc.off + 4, [[AR_BF, 16], [1, 2]]).bitcast(F32),
            tag=f"{pref}mhd")
        pmh = psv(6)
        mm(pmh[0:128, 0:16], C["ONEF"][0:1, :], MHFLAT[0:1, 0:16], True, True,
           tag=f"{pref}mhb")
        vcopy(MhB[0:128, 0:16], pmh[0:128, 0:16], tag=f"{pref}mhe")
        ar.free("ln1fm", "x1ld", "ln1t", "mhc")
        if BUILD_PHASES < 4:
            ar.free("Qb", "Kb", "Vtm", "MhB")
            continue

        ptb = ar.alloc("ptb", 128, 2 * 2048, BF16)
        ctxu = ar.alloc("ctxu", 66, 2048, BF16)
        rsum = ar.alloc("rsum", 128, 2048, F32)
        for h in range(HA):
            fq = h // 2
            prow = (h % 2) * 64
            pctx = psv(0, 4, P=65)
            for sc in range(NTC):
                sta = Kb[prow:prow + 64, fq * 2048 + sc * 128: fq * 2048 + sc * 128 + 128]
                vsta = Vtm.sl(0, 128, sc * 1040 + h * 65, 65)
                for qh in range(2):
                    pst = psv(4 + 2 * qh, 2)
                    for qs in range(2):
                        q0 = qh * 1024 + qs * 512
                        mm(pst[0:128, qs * 512:qs * 512 + 512], sta,
                           Qb[prow:prow + 64, fq * 2048 + q0: fq * 2048 + q0 + 512],
                           True, True, tag=f"{pref}at{h}.{sc}.{qh}")
                for qh in range(2):
                    pst = psv(4 + 2 * qh, 2)
                    act(ptb.sl(0, 128, qh * 2048, 1024),
                        pst[0:128, 0:1024], AF.Exp,
                        bias=MhB[0:128, h:h + 1], scale=0.125,
                        tag=f"{pref}ax{h}.{sc}.{qh}")
                for qh in range(2):
                    for qs in range(2):
                        q0 = qh * 1024 + qs * 512
                        mm(pctx[0:65, q0:q0 + 512], vsta,
                           ptb.sl(0, 128, qh * 2048 + qs * 512, 512),
                           start=(sc == 0), stop=(sc == NTC - 1),
                           tag=f"{pref}av{h}.{sc}.{qh}")
            pctx_ = psv(0, 4, P=65)
            vcopy(rsum.sl(64, 1, 0, 2048), pctx_.sl(64, 1, 0, 2048),
                  tag=f"{pref}sc{h}")
            DMA("sync", rsum.sl(0, 1, 0, 2048), rsum.sl(64, 1, 0, 2048),
                tag=f"{pref}sd{h}")
            vrecip(rsum.sl(0, 1, 0, 2048), rsum.sl(0, 1, 0, 2048),
                   tag=f"{pref}rs{h}")
            p45 = psv(4, 4)
            for qs in range(4):
                mm(p45[0:64, qs * 512:qs * 512 + 512], C["ONEF"][0:1, 0:64],
                   rsum.sl(0, 1, qs * 512, 512), True, True,
                   tag=f"{pref}rb{h}.{qs}")
            vcopy(ctxu[0:64, 0:2048], pctx_[0:64, 0:2048], tag=f"{pref}ce{h}")
            vtt(ctxu[0:64, 0:2048], ctxu[0:64, 0:2048],
                p45[0:64, 0:2048], ALU.mult, tag=f"{pref}cx{h}")
            DMA("gpsimd", bass.AP(ctx_sp.tensor, h * 64 * 2048, [[2048, 64], [1, 2048]]),
                ctxu[0:64, 0:2048], tag=f"{pref}cs{h}")
        ar.free("ptb", "ctxu", "rsum", "Qb", "Kb", "Vtm", "MhB")
        if BUILD_PHASES < 5:
            continue

        wow = ar.alloc("wow", 128, 8 * D, BF16)
        wgw = ar.alloc("wgw", 128, 8 * 8, BF16)
        load_w(wow, "W_o", D, D, tag=f"{pref}wo")
        load_w(wgw, "W_g", D, 8, tag=f"{pref}wg")
        ctxt = ar.alloc("ctxt", 128, 2 * 1024, BF16)
        attnt = ar.alloc("attnt", 128, D, F32)
        lna = ar.alloc("lna", 128, D, F32)
        lnx = ar.alloc("lnx", 128, D, F32)
        x1ld2 = ar.alloc("x1ld2", 128, 2 * D, BF16)
        ln1t2 = ar.alloc("ln1t2", 128, D, F32)
        x2t = ar.alloc("x2t", 128, D, F32)
        x2b = ar.alloc("x2b", 128, D, BF16)
        x2fm = ar.alloc("x2fm", 128, 8 * 2048, BF16)
        wgb = ar.alloc("wgb", 128, NTC * 8, F32)
        wfm = ar.alloc("wfm", 8, NTC * 128, BF16)
        gsc = ar.alloc("gsc", 128, 16, F32)
        for tc in range(NTC):
            ct = ctxt[0:128, (tc % 2) * 1024:(tc % 2) * 1024 + 1024]
            DMA("sync", ct, bass.AP(ctx_sp.tensor, tc * 128,
                                    [[2048, 128], [128 * 2048, 8], [1, 128]]),
                tag=f"{pref}cl{tc}")
            xb = x1ld2[0:128, (tc % 2) * D:(tc % 2) * D + D]
            DMA("sync", xb, hbm(x1_sp, D, tc * 128, 128), tag=f"{pref}fx1{tc}")
            pw = psv(0, 2)
            for half in range(2):
                mm(pw[0:128, half * 512:half * 512 + 512], C["ONEB"][0:1, :],
                   bass.AP(C["BOROW"], half * 512, [[D, 1], [1, 512]]),
                   True, False, tag=f"{pref}ob{tc}")
            for kc in range(8):
                sta = ctxt[0:128, (tc % 2) * 1024 + kc * 128:(tc % 2) * 1024 + kc * 128 + 128]
                for half in range(2):
                    mm(pw[0:128, half * 512:half * 512 + 512], sta,
                       wow[0:128, kc * D + half * 512: kc * D + half * 512 + 512],
                       start=False, stop=(kc == 7), tag=f"{pref}omm{tc}.{kc}")
            vcopy(attnt[0:128, 0:D], pw[0:128, 0:D], tag=f"{pref}oev{tc}")
            layernorm(lna[0:128, 0:D], attnt[0:128, 0:D], D, tag=f"{pref}la.{tc}",
                      g=C["G1R"][:, :], b=C["BE1R"][:, :], tmp=lnx[0:128, 0:D])
            layernorm(ln1t2[0:128, 0:D], xb, D, tag=f"{pref}l1r.{tc}")
            vtt(x2t[0:128, 0:D], ln1t2[0:128, 0:D], lna[0:128, 0:D], ALU.add,
                tag=f"{pref}x2.{tc}")
            vcopy(x2b[0:128, 0:D], x2t[0:128, 0:D], tag=f"{pref}x2b.{tc}")
            DMA("gpsimd", hbm(x2_sp, D, tc * 128, 128), x2b[0:128, 0:D],
                tag=f"{pref}x2s.{tc}")
            for kb in (0, 4):
                pv = psv(4 if kb == 0 else 5)
                for j in range(4):
                    kc = kb + j
                    trans(pv[0:128, j * 128:j * 128 + 128],
                          x2t[0:128, kc * 128:kc * 128 + 128], C["IDT32"][:, :],
                          tag=f"{pref}tx2{tc}.{kc}")
                for j in range(4):
                    kc = kb + j
                    o = kc * 2048 + tc * 128
                    vcopy(x2fm[0:128, o:o + 128], pv[0:128, j * 128:j * 128 + 128],
                          tag=f"{pref}ex2{tc}.{kc}")
            pg = psv(3)
            for kc in range(8):
                mm(pg[0:128, 0:8],
                   x2fm[0:128, kc * 2048 + tc * 128: kc * 2048 + tc * 128 + 128],
                   wgw[0:128, kc * 8:kc * 8 + 8], start=(kc == 0), stop=(kc == 7),
                   tag=f"{pref}gmm{tc}")
            gl = gsc[0:128, 0:8]
            vtt(gl, pg[0:128, 0:8], C["BGR"][:, :], ALU.add, tag=f"{pref}gb{tc}")
            gm = gsc[0:128, 8:9]
            vred(gm, gl, ALU.max, tag=f"{pref}gm{tc}")
            vts(gl, gl, gm, None, ALU.subtract, tag=f"{pref}gs{tc}")
            act(gl, gl, AF.Exp, tag=f"{pref}ge{tc}")
            vred(gsc[0:128, 9:10], gl, ALU.add, tag=f"{pref}gsum{tc}")
            vrecip(gsc[0:128, 10:11], gsc[0:128, 9:10], tag=f"{pref}gr{tc}")
            vts(gl, gl, gsc[0:128, 10:11], None, ALU.mult, tag=f"{pref}gp{tc}")
            vred(gsc[0:128, 11:12], gl, ALU.max, tag=f"{pref}g1{tc}")
            ismax = wgb[0:128, tc * 8:tc * 8 + 8]
            vts(ismax, gl, gsc[0:128, 11:12], None, ALU.is_ge, tag=f"{pref}im{tc}")
            vstt(ismax, ismax, -1e9, gl, ALU.mult, ALU.add, tag=f"{pref}p2{tc}")
            vred(gsc[0:128, 12:13], ismax, ALU.max, tag=f"{pref}m2{tc}")
            vts(ismax, gl, gsc[0:128, 12:13], None, ALU.is_ge, tag=f"{pref}ge2{tc}")
            vtt(wgb[0:128, tc * 8:tc * 8 + 8], gl, ismax, ALU.mult,
                tag=f"{pref}w{tc}")
            vcopy(lnsc[0:128, 0:8], wgb[0:128, tc * 8:tc * 8 + 8], tag=f"{pref}wc{tc}")
            pgf = psv(3)
            trans(pgf.bf(0, 8, 0, 128), lnsc[0:128, 0:8], C["IDT16"][:, :],
                  tag=f"{pref}wt{tc}")
            vcopy(wfm[0:8, tc * 128:tc * 128 + 128], pgf.bf(0, 8, 0, 128),
                  tag=f"{pref}wf{tc}")
        ar.free("wow", "wgw", "ctxt", "attnt", "lna", "lnx", "x1ld2", "ln1t2",
                "x2t", "x2b")
        if BUILD_PHASES < 6:
            ar.free("x2fm", "wgb", "wfm", "gsc")
            continue

        web = ar.alloc("web", 128, 16384, BF16)
        moeq = ar.alloc("moeq", 128, NTC * 256, F32)
        moebf = ar.alloc("moebf", 128, NTC * D, BF16)
        mst = ar.alloc("mst", 128, 2 * NTC, F32)
        wea = dr["W_e"]
        for fq in range(4):
            stg_ = 0
            for kc in range(8):
                for e_ in range(NE):
                    src = bass.AP(wea.tensor, (e_ * D + kc * 128) * D + fq * 256,
                                  [[D, 128], [1, 256]])
                    DMA("gpsimd", web[0:128, stg_ * 16384 + kc * 2048 + e_ * 256:
                                      stg_ * 16384 + kc * 2048 + e_ * 256 + 256],
                        src, tag=f"{pref}we{fq}.{kc}.{e_}")
            for tc in range(NTC):
                tp_ = (tc % 2) * 256
                pb = psv(0)
                mm(pb[0:128, tp_:tp_ + 256], wfm[0:8, tc * 128:tc * 128 + 128],
                   bass.AP(C["BEXP"], fq * 256, [[D, 8], [1, 256]]), True, True,
                   tag=f"{pref}bx{fq}.{tc}")
                vcopy(moeq[0:128, tc * 256:tc * 256 + 256], pb[0:128, tp_:tp_ + 256],
                      tag=f"{pref}bv{fq}.{tc}")
                for kc in range(8):
                    sta = x2fm[0:128, kc * 2048 + tc * 128: kc * 2048 + tc * 128 + 128]
                    for e_ in range(NE):
                        mm(psv(e_)[0:128, tp_:tp_ + 256],
                           sta,
                           web[0:128, stg_ * 16384 + kc * 2048 + e_ * 256:
                               stg_ * 16384 + kc * 2048 + e_ * 256 + 256],
                           start=(kc == 0), stop=(kc == 7),
                           tag=f"{pref}em{fq}.{tc}.{e_}")
                for e_ in range(NE):
                    vstt(moeq[0:128, tc * 256:tc * 256 + 256],
                         psv(e_)[0:128, tp_:tp_ + 256],
                         wgb[0:128, tc * 8 + e_:tc * 8 + e_ + 1],
                         moeq[0:128, tc * 256:tc * 256 + 256], ALU.mult, ALU.add,
                         tag=f"{pref}ev{fq}.{tc}.{e_}")
            for tc in range(NTC):
                sl_ = moeq[0:128, tc * 256:tc * 256 + 256]
                dst = moebf[0:128, tc * D + fq * 256: tc * D + fq * 256 + 256]
                t1 = lnst[0:128, 6:7]
                t2 = lnst[0:128, 7:8]
                act(dst, sl_, AF.Copy, accum=t1, tag=f"{pref}mc{fq}.{tc}")
                act(lnsc[0:128, 0:256], sl_, AF.Square, accum=t2,
                    tag=f"{pref}ms{fq}.{tc}")
                if fq == 0:
                    vcopy(mst[0:128, tc:tc + 1], t1, tag=f"{pref}m1{fq}.{tc}")
                    vcopy(mst[0:128, NTC + tc:NTC + tc + 1], t2, tag=f"{pref}m2{fq}.{tc}")
                else:
                    vtt(mst[0:128, tc:tc + 1], mst[0:128, tc:tc + 1], t1, ALU.add,
                        tag=f"{pref}m1{fq}.{tc}")
                    vtt(mst[0:128, NTC + tc:NTC + tc + 1],
                        mst[0:128, NTC + tc:NTC + tc + 1], t2, ALU.add,
                        tag=f"{pref}m2{fq}.{tc}")
        ar.free("web", "moeq", "x2fm", "wgb", "wfm", "gsc")
        if BUILD_PHASES < 7:
            ar.free("moebf", "mst")
            continue

        x1ld3 = ar.alloc("x1ld3", 128, 2 * D, BF16)
        x2ld3 = ar.alloc("x2ld3", 128, 2 * D, BF16)
        ffnt = ar.alloc("ffnt", 128, D, F32)
        lnf = ar.alloc("lnf", 128, D, F32)
        outt = ar.alloc("outt", 128, 2 * D, F32)
        for tc in range(NTC):
            xb1 = x1ld3[0:128, (tc % 2) * D:(tc % 2) * D + D]
            xb2 = x2ld3[0:128, (tc % 2) * D:(tc % 2) * D + D]
            DMA("sync", xb1, hbm(x1_sp, D, tc * 128, 128), tag=f"{pref}ix1{tc}")
            DMA("sync", xb2, hbm(x2_sp, D, tc * 128, 128), tag=f"{pref}ix2{tc}")
            dp = f"{pref}fin{tc}"
            m = lnst[0:128, 8:9]
            vv = lnst[0:128, 9:10]
            r = lnst[0:128, 10:11]
            tt = lnst[0:128, 11:12]
            vts(m, mst[0:128, tc:tc + 1], 1.0 / D, None, ALU.mult, tag=f"{dp}.m")
            vtt(tt, m, m, ALU.mult, tag=f"{dp}.mm")
            vstt(vv, mst[0:128, NTC + tc:NTC + tc + 1], 1.0 / D, tt, ALU.mult,
                 ALU.subtract, tag=f"{dp}.v")
            act(r, vv, AF.Ln, bias=EPS, tag=f"{dp}.l")
            act(r, r, AF.Exp, scale=-0.5, tag=f"{dp}.e")
            vts(ffnt[0:128, 0:D], moebf[0:128, tc * D:tc * D + D], m, r,
                ALU.subtract, ALU.mult, tag=f"{dp}.n")
            vtt(ffnt[0:128, 0:D], ffnt[0:128, 0:D], C["G2R"][:, :], ALU.mult,
                tag=f"{dp}.g")
            vtt(ffnt[0:128, 0:D], ffnt[0:128, 0:D], C["BE2R"][:, :], ALU.add,
                tag=f"{dp}.b")
            vtt(ffnt[0:128, 0:D], ffnt[0:128, 0:D], xb2, ALU.add, tag=f"{dp}.x2")
            layernorm(lnf[0:128, 0:D], ffnt[0:128, 0:D], D, tag=f"{dp}.lo")
            ot = outt[0:128, (tc % 2) * D:(tc % 2) * D + D]
            vtt(ot, lnf[0:128, 0:D], xb1, ALU.add, tag=f"{dp}.o")
            DMA("sync", hbm(out, D, tok0 + tc * 128, 128), ot, tag=f"{dp}.st")
        ar.free("x1ld3", "x2ld3", "ffnt", "lnf", "outt", "moebf", "mst")

    sems = {}
    for e in ENGS:
        sems[f"x_{e}"] = stack.enter_context(nc.semaphore(f"x_{e}"))
        for l in range(4):
            sems[f"d_{e}{l}"] = stack.enter_context(nc.semaphore(f"d_{e}{l}"))
    with nc.Block() as blk:
        sch.materialize(nc, blk, sems)
    stack.close()
    return nc


_CACHED = {}


def kernel(**inputs):
    W = host_prep(inputs)
    u_full = np.ascontiguousarray(np.asarray(inputs["L0_attn_u"], np.float32))
    key = "prog"
    if key not in _CACHED:
        _CACHED[key] = build_program()
    nc = _CACHED[key]
    in_maps = []
    for c in range(8):
        m = {k: W[k] for k in HOST_KEYS}
        m["u"] = np.ascontiguousarray(u_full[2 * c:2 * c + 2].reshape(2 * S, D))
        in_maps.append(m)
    res = run_bass_kernel_spmd(nc, in_maps, core_ids=list(range(8)))
    outs = [np.asarray(r["out"], np.float32).reshape(2, S, D) for r in res.results]
    return np.concatenate(outs, axis=0)
